# version 13
# baseline (speedup 1.0000x reference)
import numpy as np
import ml_dtypes

import concourse.bass as bass
import concourse.mybir as mybir
from concourse.bass_utils import run_bass_kernel_spmd
from concourse.bass import broadcast_tensor_aps

F32 = mybir.dt.float32
BF16 = mybir.dt.bfloat16
AF = mybir.ActivationFunctionType
ALU = mybir.AluOpType
AX = mybir.AxisListType

D = 1024
S = 4096
NCH = 8
H = 16
DH = 64
E = 8
HID = 512
T = 256
NT = S // T
NWIN = 30
LC = S - 1

ENGS = ("sync", "tensor", "vector", "scalar", "gpsimd")


class Prog:

    def __init__(self, nc):
        self.nc = nc
        self.q = {e: [] for e in ENGS}
        self.sems = {}
        self.cnt = {}
        for name in ("tensor", "vector", "scalar", "gpsimd", "dsync", "dscalar", "dgp"):
            self.sems[name] = nc.alloc_semaphore("sem_" + name)
            self.cnt[name] = 0

    def _wait(self, eng, ms):
        sn, v = ms
        if v <= 0:
            return
        assert v <= self.cnt[sn], f"{eng} waits {sn}>={v}, emitted {self.cnt[sn]}"
        s = self.sems[sn]
        self.q[eng].append(lambda e, s=s, v=v: e.wait_ge(s, v))

    def op(self, eng, fn, waits=(), inc=None):
        serial = eng in ("vector", "scalar", "gpsimd")
        if serial:
            self._wait(eng, (eng, self.cnt[eng]))
        for ms in waits:
            if ms is not None:
                self._wait(eng, ms)
        if serial or inc:
            s = self.sems[eng]
            self.cnt[eng] += 1
            n = self.cnt[eng]
            self.q[eng].append(lambda e, s=s, fn=fn: fn(e).then_inc(s, 1))
            return (eng, n)
        self.q[eng].append(fn)
        return (eng, 0)

    def dma(self, eng, fn, waits=()):
        dname = {"sync": "dsync", "scalar": "dscalar", "gpsimd": "dgp"}[eng]
        if eng in ("scalar", "gpsimd"):
            self._wait(eng, (eng, self.cnt[eng]))
        for ms in waits:
            if ms is not None:
                self._wait(eng, ms)
        s = self.sems[dname]
        self.cnt[dname] += 16
        n = self.cnt[dname]
        self.q[eng].append(lambda e, s=s, fn=fn: fn(e).then_inc(s, 16))
        return (dname, n)

    def mark(self, eng):
        return (eng, self.cnt[eng])

    def run_block(self):
        with self.nc.Block() as block:
            block.sync(lambda e: [f(e) for f in self.q["sync"]])
            block.tensor(lambda e: [f(e) for f in self.q["tensor"]])
            block.vector(lambda e: [f(e) for f in self.q["vector"]])
            block.scalar(lambda e: [f(e) for f in self.q["scalar"]])
            block.gpsimd(lambda e: [f(e) for f in self.q["gpsimd"]])


def bcast_mid(big3, small2):
    s3 = small2.rearrange("p (a t) -> p a t", a=1)
    return broadcast_tensor_aps(big3, s3)[1]


def build_nc(stop_after=3, parts="ABC", aparts="ABCD"):
    nc = bass.Bass("TRN2", target_bir_lowering=False, dynamic_dma_scratch_size=16384,
                   detect_race_conditions=False)
    dram = {}

    def din(name, shape, dt=F32):
        dram[name] = nc.dram_tensor(name, list(shape), dt, kind="ExternalInput")
        return dram[name]

    x_in = din("x_fm", (NT, 128, NCH * T))
    din("Wq", (D, D)); din("Wk", (D, D)); din("Wv", (D, D)); din("Wo", (D, D))
    din("bq", (128, NCH)); din("bk", (128, NCH)); din("bo", (128, NCH))
    din("bv_row", (1, D))
    din("gateW", (128, NCH * E))
    din("W1", (E, D, HID)); din("W2", (E, HID, D))
    din("b1", (128, E * 4)); din("b2", (E, D))
    din("Ws1", (D, HID)); din("Ws2", (HID, D))
    din("bs1", (128, 4)); din("bs2", (128, NCH))
    din("rs", (1, 1))
    din("ones_r128_f32", (1, 128))
    din("ones_r128_bf16", (1, 128), BF16)
    din("ones_r8_f32", (1, 8))
    din("ones_c_bf16", (128, 1), BF16)
    din("onehot8", (E, E * 128), BF16)
    din("id_f32", (128, 128))
    din("mask16", (H, D))

    out_fm = nc.dram_tensor("out_fm", [NT, 128, NCH * T], F32, kind="ExternalOutput")

    x1_hbm = nc.dram_tensor("x1_s", [NT, 128, NCH * T], F32, kind="Internal")
    x2_hbm = nc.dram_tensor("x2_s", [NT, 128, NCH * T], F32, kind="Internal")
    xln_hbm = nc.dram_tensor("xln_s", [NT, 128, NCH * T], BF16, kind="Internal")
    qs_hbm = nc.dram_tensor("qs_s", [D, S], BF16, kind="Internal")
    ks_hbm = nc.dram_tensor("ks_s", [D, S], BF16, kind="Internal")
    vs_hbm = nc.dram_tensor("vs_s", [S, D], BF16, kind="Internal")
    rows_hbm = nc.dram_tensor("rows_s", [1, S], F32, kind="Internal")
    nmh_hbm = nc.dram_tensor("nmh_s", [1, S], BF16, kind="Internal")

    wbuf1 = nc.alloc_sbuf_tensor("wbuf1", [128, E * NCH * HID], BF16)
    wbuf2 = nc.alloc_sbuf_tensor("wbuf2", [128, E * 4 * D], BF16)
    wbufS = nc.alloc_sbuf_tensor("wbufS", [128, NCH * HID + 4 * D], BF16)

    def W1v(e, kc, hc):
        base = (e * NCH + kc) * HID + hc * 128
        return wbuf1[:, base:base + 128]

    def W2v(e, kc, mc):
        base = (e * 4 + kc) * D + mc * 128
        return wbuf2[:, base:base + 128]

    def Ws1v(kc, hc):
        return wbufS[:, kc * HID + hc * 128:kc * HID + hc * 128 + 128]

    def Ws2v(kc, mc):
        base = NCH * HID + kc * D + mc * 128
        return wbufS[:, base:base + 128]

    def Wqv(kc, mc):
        return wbuf1[:, (0 * NCH + kc) * D + mc * 128:][:, 0:128]

    def Wkv(kc, mc):
        return wbuf1[:, (NCH + kc) * D + mc * 128:][:, 0:128]

    def Wvv(kc, half):
        return wbuf1[:, (2 * NCH + kc) * D + half * 512:][:, 0:512]

    def Wov(kc, mc):
        return wbuf2[:, kc * D + mc * 128:][:, 0:128]

    c_bq8 = nc.alloc_sbuf_tensor("c_bq8", [128, NCH], F32)
    c_bk = nc.alloc_sbuf_tensor("c_bk", [128, NCH], F32)
    c_bo = nc.alloc_sbuf_tensor("c_bo", [128, NCH], F32)
    c_bs2h = nc.alloc_sbuf_tensor("c_bs2h", [128, NCH], F32)
    c_b1 = nc.alloc_sbuf_tensor("c_b1", [128, E * 4], F32)
    c_bs1 = nc.alloc_sbuf_tensor("c_bs1", [128, 4], F32)
    c_b2bf = nc.alloc_sbuf_tensor("c_b2bf", [E, D], BF16)
    c_gw = nc.alloc_sbuf_tensor("c_gw", [128, NCH * E], BF16)
    c_rs = nc.alloc_sbuf_tensor("c_rs", [128, 1], F32)
    c_ones_r128 = nc.alloc_sbuf_tensor("c_ones_r128", [1, 128], F32)
    c_ones_r128b = nc.alloc_sbuf_tensor("c_ones_r128b", [1, 128], BF16)
    c_ones_c = nc.alloc_sbuf_tensor("c_ones_c", [128, 1], BF16)
    c_oh8 = nc.alloc_sbuf_tensor("c_oh8", [E, E * 128], BF16)
    c_id = nc.alloc_sbuf_tensor("c_id", [128, 128], F32)
    c_q0 = nc.alloc_sbuf_tensor("c_q0", [128, NCH], BF16)
    c_k0 = nc.alloc_sbuf_tensor("c_k0", [128, NCH], BF16)
    gbuf = nc.alloc_sbuf_tensor("gbuf", [E, S], BF16)

    AFA = nc.alloc_sbuf_tensor("AFA", [128, 8192], F32)
    ABF = nc.alloc_sbuf_tensor("ABF", [128, 9216], BF16)

    def xa(i):
        return AFA[:, (i % 2) * 2048:(i % 2) * 2048 + 2048]

    def xa3(i):
        return xa(i).rearrange("p (c t) -> p c t", c=NCH)

    tevac = AFA[:, 4096:6144]
    tevac3 = tevac.rearrange("p (c t) -> p c t", c=NCH)

    def xlnb(i):
        return ABF[:, (i % 2) * 2048:(i % 2) * 2048 + 2048]

    def xlnb3(i):
        return xlnb(i).rearrange("p (c t) -> p c t", c=NCH)

    h_g = ABF[:, 4096:5120]
    h_s = ABF[:, 5120:6144]
    s1_sb = ABF[:, 6144:7168]
    gb_sb = ABF[:, 7168:9216]

    PB = nc.alloc_psum_tensor("PB", [128, 4096], F32)
    pb = [PB[:, i * 512:(i + 1) * 512] for i in range(8)]

    def fm_tile(hbm, i):
        return hbm.ap()[i]

    p = Prog(nc)

    ld_bq = p.dma("sync", lambda e: e.dma_start(out=AFA[:, 64:64 + NCH], in_=dram["bq"][:]))
    ld_bs2 = p.dma("sync", lambda e: e.dma_start(out=AFA[:, 96:96 + NCH], in_=dram["bs2"][:]))
    for name, dst in [("bk", c_bk), ("bo", c_bo), ("b1", c_b1), ("bs1", c_bs1),
                      ("ones_r128_f32", c_ones_r128), ("ones_r128_bf16", c_ones_r128b),
                      ("id_f32", c_id),
                      ("ones_c_bf16", c_ones_c), ("onehot8", c_oh8)]:
        p.dma("sync", lambda e, d=dst, s=dram[name]: e.dma_start(out=d[:], in_=s[:]))
    ld_ones = p.mark("dsync")
    gw_ready = p.dma("gpsimd", lambda e: e.dma_start(out=c_gw[:], in_=dram["gateW"][:]))
    b2_ready = p.dma("gpsimd", lambda e: e.dma_start(out=c_b2bf[:], in_=dram["b2"][:]))
    ld_rs = p.dma("sync", lambda e: e.dma_start(out=AFA[0:1, 128:129], in_=dram["rs"][:]))
    ld_or8 = p.dma("sync", lambda e: e.dma_start(out=AFA[0:1, 136:144], in_=dram["ones_r8_f32"][:]))
    mrs = p.op("tensor", lambda e: e.matmul(pb[0][0:8, 0:1], AFA[0:1, 136:144],
                                            AFA[0:1, 128:129], start=True, stop=True),
               waits=[ld_rs, ld_or8], inc=True)
    p.op("vector", lambda e: e.tensor_copy(c_rs[0:8, 0:1], pb[0][0:8, 0:1]), waits=[mrs])
    p.op("vector", lambda e: e.tensor_scalar_mul(c_bq8[:], AFA[:, 64:64 + NCH], 0.125),
         waits=[ld_bq])
    p.op("vector", lambda e: e.tensor_scalar_mul(c_bs2h[:], AFA[:, 96:96 + NCH], 0.5),
         waits=[ld_bs2])
    p0_done_v = p.mark("vector")

    def stream_w1(waits=()):
        rl = []
        for ee in range(E):
            for half in range(2):
                src = dram["W1"].ap()[ee][half * 512:half * 512 + 512].rearrange(
                    "(kc p) h -> p kc h", p=128)
                base = ee * NCH * HID + half * 4 * HID
                dst = wbuf1[:, base:base + 4 * HID].rearrange(
                    "p (kc h) -> p kc h", kc=4)
                r = p.dma("gpsimd", lambda e, src=src, dst=dst: e.dma_start(
                    out=dst, in_=src), waits=waits if ee == 0 and half == 0 else [])
            rl.append(r)
        return rl

    def stream_w2(waits=()):
        rl = []
        for ee in range(E):
            src = dram["W2"].ap()[ee].rearrange("(kc p) d -> p kc d", p=128)
            dst = wbuf2[:, ee * 4 * D:(ee + 1) * 4 * D].rearrange(
                "p (kc d) -> p kc d", kc=4)
            rl.append(p.dma("gpsimd", lambda e, src=src, dst=dst: e.dma_start(
                out=dst, in_=src), waits=waits if ee == 0 else []))
        return rl

    def stream_ws(waits=()):
        for half in range(2):
            s1 = dram["Ws1"].ap()[half * 512:half * 512 + 512].rearrange(
                "(kc p) h -> p kc h", p=128)
            d1 = wbufS[:, half * 4 * HID:(half + 1) * 4 * HID].rearrange(
                "p (kc h) -> p kc h", kc=4)
            p.dma("gpsimd", lambda e, s1=s1, d1=d1: e.dma_start(out=d1, in_=s1),
                  waits=waits)
        s2 = dram["Ws2"].ap().rearrange("(kc p) d -> p kc d", p=128)
        d2 = wbufS[:, NCH * HID:].rearrange("p (kc d) -> p kc d", kc=4)
        return p.dma("gpsimd", lambda e: e.dma_start(out=d2, in_=s2), waits=waits)

    def stream_attn(waits=()):
        r = None
        for name, off, buf in [("Wq", 0, wbuf1), ("Wk", NCH * D, wbuf1),
                               ("Wv", 2 * NCH * D, wbuf1), ("Wo", 0, wbuf2)]:
            for half in range(2):
                src = dram[name].ap()[half * 512:half * 512 + 512].rearrange(
                    "(kc p) m -> p kc m", p=128)
                b2 = off + half * 4 * D
                r = p.dma("gpsimd", lambda e, b2=b2, buf=buf, src=src: e.dma_start(
                    out=buf[:, b2:b2 + 4 * D].rearrange("p (kc m) -> p kc m", kc=4),
                    in_=src), waits=waits)
        return r

    ws_ready = stream_ws()
    w1_ready = stream_w1()
    w2_ready = stream_w2()

    def emit_ln_head(i, src_hbm, hist, first_waits, xa_dep,
                     xb_deps=(), xq_deps=()):
        ld = p.dma("sync", lambda e, i=i: e.dma_start(
            out=xa(i)[:], in_=fm_tile(src_hbm, i)),
            waits=list(first_waits) if i < 2 else [xa_dep])
        hist["ld"][i] = ld
        xb = ABF[:, (i % 2) * 2048:(i % 2) * 2048 + 2048]
        xq = ABF[:, 4096 + (i % 2) * 2048:4096 + (i % 2) * 2048 + 2048]
        cp = p.op("scalar", lambda e, xb=xb, i=i: e.activation(xb[:], xa(i)[:], AF.Copy),
                  waits=[ld, hist["mm_s"].get(i - 2)] + list(xb_deps))
        sq = p.op("scalar", lambda e, xq=xq, i=i: e.activation(xq[:], xa(i)[:], AF.Square),
                  waits=[ld, hist["mm_q"].get(i - 2)] + list(xq_deps))
        ps_s = pb[6 + i % 2][0:1, 0:T]
        ps_q = pb[6 + i % 2][0:1, T:2 * T]
        for c in range(NCH):
            mm = p.op("tensor", lambda e, c=c, xb=xb, ps=ps_s: e.matmul(
                ps, c_ones_c[:], xb[:, c * T:(c + 1) * T],
                start=(c == 0), stop=False, skip_group_check=True),
                waits=[cp, hist["e2"].get(i - 2)] if c == 0 else [],
                inc=(c == NCH - 1))
        hist["mm_s"][i] = mm
        for c in range(NCH):
            mm = p.op("tensor", lambda e, c=c, xq=xq, ps=ps_q: e.matmul(
                ps, c_ones_c[:], xq[:, c * T:(c + 1) * T],
                start=False, stop=(c == NCH - 1), skip_group_check=True),
                waits=[sq] if c == 0 else [],
                inc=(c == NCH - 1))
        hist["mm_q"][i] = mm

    def emit_ln_tail(i, hist, xln_dep, bm_extra, br_extra):
        ps_s = pb[6 + i % 2][0:1, 0:T]
        ps_q = pb[6 + i % 2][0:1, T:2 * T]
        nm = AFA[0:1, 7168 + (i % 2) * T:7168 + (i % 2) * T + T]
        vr = AFA[0:1, 7680 + (i % 2) * T:7680 + (i % 2) * T + T]
        hist["e1"][i] = p.op("scalar", lambda e, nm=nm, ps=ps_s: e.activation(
            nm, ps, AF.Copy, scale=-1.0 / D),
            waits=[hist["mm_q"][i], hist["br"].get(i - 2)])
        mb, rb = pb[4][:, 0:T], pb[5][:, 0:T]
        hist["bm"][i] = p.op("tensor", lambda e, nm=nm, mb=mb: e.matmul(
            mb, c_ones_r128[:], nm, start=True, stop=True, skip_group_check=True),
            waits=[hist["e1"][i], hist["tt1"].get(i - 1)] + list(bm_extra(i)),
            inc=True)
        hist["e2"][i] = p.op("scalar", lambda e, vr=vr, ps=ps_q: e.activation(
            vr, ps, AF.Copy, scale=1.0 / D),
            waits=[hist["br"].get(i - 2)])
        p.op("vector", lambda e, nm=nm: e.tensor_mul(nm, nm, nm),
             waits=[hist["bm"][i]])
        vv = p.op("vector", lambda e, vr=vr, nm=nm: e.tensor_sub(vr, vr, nm),
                  waits=[hist["e2"][i]])
        vv = p.op("vector", lambda e, vr=vr: e.tensor_scalar_add(vr, vr, 1e-5))
        sx = p.op("scalar", lambda e, vr=vr, nm=nm: e.activation(nm, vr, AF.Sqrt),
                  waits=[vv])
        hist["rr"][i] = sx
        hist["br"][i] = p.op("tensor", lambda e, nm=nm, rb=rb: e.matmul(
            rb, c_ones_r128[:], nm, start=True, stop=True, skip_group_check=True),
            waits=[sx, hist["tt2"].get(i - 1)] + list(br_extra(i)), inc=True)
        rc2 = p.op("vector", lambda e, rb=rb: e.reciprocal(rb, rb),
                   waits=[hist["br"][i]])
        hist["tt1"][i] = p.op("vector", lambda e, i=i, mb=mb: e.tensor_tensor(
            tevac3, xa3(i), bcast_mid(tevac3, mb), ALU.add),
            waits=[hist["ld"][i], hist["bm"][i]])
        hist["tt2"][i] = p.op("vector", lambda e, i=i, rb=rb: e.tensor_tensor(
            xlnb3(i), tevac3, bcast_mid(tevac3, rb), ALU.mult),
            waits=[xln_dep])
        return hist["tt2"][i]

    def ffn_pass(src_hbm, dst_hbm, first_waits, parts="ABC", wdeps=None):
        w1r, w2r, wsr = wdeps if wdeps else (w1_ready, w2_ready, ws_ready)
        hist = {k: {} for k in ("mm_s", "mm_q", "e1", "e2", "bm", "br",
                                "tt1", "tt2", "rr", "ld")}
        vm = next((w for w in first_waits if w and w[0] == "vector"), None)
        hist["tt2"][-1] = vm
        hist["tt1"][-1] = vm
        stx_h, sg_h, gd_h = {}, {}, {}
        ev_pb1 = None
        gd_prev = None
        gmm_h = {}

        def ffn_head(k):
            if k < NT:
                emit_ln_head(k, src_hbm, hist, first_waits,
                             hist["tt1"].get(k - 2),
                             xb_deps=[stx_h.get(k - 2), gmm_h.get(k - 2)])
        for k in range(NT + 2):
            if k < 2:
                ffn_head(k)
                continue
            i = k - 2
            tt2 = emit_ln_tail(i, hist, stx_h.get(i - 2),
                               lambda i: [], lambda i: [])
            stx_h[i] = p.dma("sync", lambda e, i=i: e.dma_start(
                out=fm_tile(xln_hbm, i), in_=xlnb(i)[:]), waits=[tt2])
            for c in range(NCH):
                mm = p.op("tensor", lambda e, c=c, i=i: e.matmul(
                    pb[0][0:8, 0:T], c_gw[:, c * E:(c + 1) * E],
                    xlnb(i)[:, c * T:(c + 1) * T],
                    start=(c == 0), stop=(c == NCH - 1), skip_group_check=True),
                    waits=[tt2, gw_ready, sg_h.get(i - 1)] if c == 0 else [],
                    inc=(c == NCH - 1))
            gmm_h[i] = mm
            ssb = AFA[0:8, 6144:6144 + T]
            sg_h[i] = p.op("scalar", lambda e, ssb=ssb: e.activation(
                ssb, pb[0][0:8, 0:T], AF.Sigmoid), waits=[mm])
            ts = p.op("vector", lambda e, ssb=ssb: e.tensor_scalar_mul(
                ssb, ssb, c_rs[0:8, 0:1]), waits=[sg_h[i], p0_done_v])
            for tc in range(T // 128):
                sl = ssb[:, tc * 128:(tc + 1) * 128]
                tr = p.op("tensor", lambda e, sl=sl, tc=tc: e.transpose(
                    pb[1][:, tc * 8:tc * 8 + 8], sl, c_id[0:8, 0:8]),
                    waits=[ts, ev_pb1], inc=True)
                sc_t = AFA[:, 6400 + tc * 8:6400 + tc * 8 + 8]
                ev_pb1 = p.op("vector", lambda e, sc_t=sc_t, tc=tc: e.tensor_copy(
                    sc_t, pb[1][:, tc * 8:tc * 8 + 8]), waits=[tr])
                top8 = AFA[:, 6416 + tc * 8:6416 + tc * 8 + 8]
                p.op("vector", lambda e, top8=top8, sc_t=sc_t: e.max(top8, sc_t))
                dsum = AFA[:, 6436 + tc * 2:6436 + tc * 2 + 1]
                dinv = AFA[:, 6437 + tc * 2:6437 + tc * 2 + 1]
                p.op("vector", lambda e, dsum=dsum, top8=top8: e.tensor_add(
                    dsum, top8[:, 0:1], top8[:, 1:2]))
                p.op("vector", lambda e, dinv=dinv, dsum=dsum: e.reciprocal(dinv, dsum))
                p.op("vector", lambda e, top8=top8: e.memset(top8[:, 2:8], -1e30))
                mrp = AFA[:, 6444 + tc * 8:6444 + tc * 8 + 8]
                p.op("vector", lambda e, mrp=mrp, top8=top8, sc_t=sc_t: e.match_replace(
                    mrp, top8, sc_t, 0.0))
                p.op("vector", lambda e, mrp=mrp, sc_t=sc_t: e.tensor_sub(mrp, sc_t, mrp))
                gt = p.op("vector", lambda e, mrp=mrp, dinv=dinv: e.tensor_scalar_mul(
                    mrp, mrp, dinv))
                tr2 = p.op("tensor", lambda e, mrp=mrp, tc=tc: e.transpose(
                    pb[2][0:8, tc * 128:tc * 128 + 128], mrp, c_id[:]),
                    waits=[gt, gd_prev], inc=True)
                gd_prev = p.op("vector", lambda e, tc=tc, i=i: e.tensor_copy(
                    gbuf[0:8, i * T + tc * 128:i * T + tc * 128 + 128],
                    pb[2][0:8, tc * 128:tc * 128 + 128]), waits=[tr2])
            gd_h[i] = gd_prev
            ffn_head(k)
        def hv(par, c):
            return pb[par * 2 + c // 2][:, (c % 2) * T:(c % 2) * T + T]

        def ov(c):
            return pb[4 + c // 2][:, (c % 2) * T:(c % 2) * T + T]

        st_c = {}
        xln_use = {}
        ev_prev = None
        gl_done = [None, None]
        gm_done = [None, None]
        ffn2_done = {}
        if parts == "ABC1a":
            pass
        NTC = 1 if parts in ("ABC1", "ABC1a", "ABC1b", "ABC1c") else NT if "C" in parts else int(parts[3:]) if parts.startswith("ABC") and len(parts) > 3 else NT
        for i in range(NTC):
            ldx = p.dma("sync", lambda e, i=i: e.dma_start(out=xa(i)[:], in_=fm_tile(src_hbm, i)),
                        waits=[p.mark("vector")] if i < 2 else [st_c[i - 2]])
            ldl = p.dma("sync", lambda e, i=i: e.dma_start(
                out=xlnb(i)[:], in_=fm_tile(xln_hbm, i)),
                waits=[stx_h[NT - 1]] if i < 2 else [xln_use[i - 2]])
            w0 = [gd_h[i], b2_ready]
            if i == 0:
                w0 += [p.mark("vector"), p.mark("scalar")]
            else:
                w0 += [gl_done[0], gl_done[1], gm_done[0], gm_done[1]]
            for ee in range(E):
                mm = p.op("tensor", lambda e, ee=ee, i=i: e.matmul(
                    hv(ee // 4, ee % 4), c_oh8[:, ee * 128:(ee + 1) * 128],
                    gbuf[0:8, i * T:(i + 1) * T], start=True, stop=True,
                    skip_group_check=True),
                    waits=w0 if ee == 0 else [], inc=(ee == E - 1))
            for ee in range(E):
                p.op("vector", lambda e, ee=ee: e.tensor_copy(
                    gb_sb[:, ee * T:(ee + 1) * T], hv(ee // 4, ee % 4)), waits=[mm])
            gb_done = p.mark("vector")
            for hc in range(4):
                for kc in range(NCH):
                    mm = p.op("tensor", lambda e, hc=hc, kc=kc, i=i: e.matmul(
                        hv(0, hc), Ws1v(kc, hc), xlnb(i)[:, kc * T:(kc + 1) * T],
                        start=(kc == 0 and hc % 2 == 0), stop=False,
                        skip_group_check=True),
                        waits=[gb_done, ldl, wsr] if (hc == 0 and kc == 0) else [],
                        inc=(hc == 3 and kc == NCH - 1))
            s1_mm = mm
            for hc in range(4):
                sg1 = p.op("scalar", lambda e, hc=hc: e.activation(
                    s1_sb[:, hc * T:(hc + 1) * T], hv(0, hc), AF.Gelu,
                    bias=c_bs1[:, hc:hc + 1]),
                    waits=[s1_mm] if hc == 0 else [])
            if parts == "ABC1a":
                dbg = [p.mark("vector"), p.mark("scalar")]
                p.dma("gpsimd", lambda e: e.dma_start(
                    out=dst_hbm.ap()[1], in_=gb_sb[:]), waits=dbg)
                p.dma("gpsimd", lambda e: e.dma_start(
                    out=dst_hbm.ap()[2][:, 0:1024], in_=s1_sb[:]), waits=dbg)
                p.dma("gpsimd", lambda e: e.dma_start(
                    out=dst_hbm.ap()[5][:, 0:2048], in_=xlnb(0)[:]), waits=dbg)
                p._wait("sync", p.mark("dgp"))
                return None, None, None, None
            for c in range(NCH):
                for kc in range(4):
                    mm = p.op("tensor", lambda e, c=c, kc=kc: e.matmul(
                        ov(c), Ws2v(kc, c), s1_sb[:, kc * T:(kc + 1) * T],
                        start=(kc == 0 and c % 2 == 0), stop=False,
                        skip_group_check=True),
                        waits=[sg1, ev_prev] if (c == 0 and kc == 0) else [],
                        inc=(c == NCH - 1 and kc == 3))
            s2_mm = mm
            for c in range(NCH):
                p.op("tensor", lambda e, c=c, i=i: e.matmul(
                    ov(c), c_b2bf[:, c * 128:(c + 1) * 128],
                    gbuf[0:8, i * T:(i + 1) * T], start=False, stop=False,
                    skip_group_check=True))

            if parts == "ABC1b":
                for c in range(NCH):
                    ev = p.op("scalar", lambda e, c=c: e.activation(
                        tevac[:, c * T:(c + 1) * T], ov(c), AF.Identity,
                        bias=c_bs2h[:, c:c + 1], scale=0.5),
                        waits=[s2_mm, p.mark("tensor")] if c == 0 else [])
                p.dma("sync", lambda e: e.dma_start(
                    out=dst_hbm.ap()[4], in_=tevac[:]), waits=[ev])
                p._wait("sync", p.mark("dsync"))
                return None, None, None, None

            def emit_ffn1(ee):
                par = ee % 2
                w = []
                if ee == 0:
                    w = [sg1, gb_done]
                elif ee == 1:
                    w = [gb_done]
                elif gl_done[par] is not None:
                    w = [gl_done[par]]
                mm = None
                for hc in range(4):
                    for kc in range(NCH):
                        mm = p.op("tensor", lambda e, ee=ee, hc=hc, kc=kc, i=i: e.matmul(
                            hv(par, hc), W1v(ee, kc, hc),
                            xlnb(i)[:, kc * T:(kc + 1) * T],
                            start=(kc == 0 and hc % 2 == 0), stop=False,
                            skip_group_check=True),
                            waits=(w + [w1r[ee]]) if (hc == 0 and kc == 0) else [],
                            inc=(hc == 3 and kc == NCH - 1))
                return mm

            def emit_tail(ee):
                par = ee % 2
                for hc in range(4):
                    gl = p.op("scalar", lambda e, ee=ee, hc=hc, par=par: e.activation(
                        h_g[:, hc * T:(hc + 1) * T], hv(par, hc), AF.Gelu,
                        bias=c_b1[:, ee * 4 + hc:ee * 4 + hc + 1]),
                        waits=([h1_mm[ee]] + ([gm_done[1 - par]] if gm_done[1 - par] else []))
                        if hc == 0 else [])
                gl_done[par] = gl
                h_g3 = h_g.rearrange("p (c t) -> p c t", c=4)
                h_s3 = h_s.rearrange("p (c t) -> p c t", c=4)
                gm = p.op("vector", lambda e, ee=ee: e.tensor_tensor(
                    h_s3, h_g3, bcast_mid(h_g3, gb_sb[:, ee * T:(ee + 1) * T]), ALU.mult),
                    waits=[gl, ffn2_done.get(ee - 1)])
                gm_done[par] = gm
                mm = None
                for c in range(NCH):
                    for kc in range(4):
                        mm = p.op("tensor", lambda e, ee=ee, c=c, kc=kc: e.matmul(
                            ov(c), W2v(ee, kc, c), h_s[:, kc * T:(kc + 1) * T],
                            start=False, stop=(ee == E - 1 and c == NCH - 1 and kc == 3),
                            skip_group_check=True),
                            waits=([gm, w2r[ee]] if (c == 0 and kc == 0) else []),
                            inc=(c == NCH - 1 and kc == 3))
                ffn2_done[ee] = mm

            h1_mm = {}
            h1_mm[0] = emit_ffn1(0)
            if parts == "ABC1c":
                emit_tail(0)
                acc1 = ffn2_done[0]
                for c in range(NCH):
                    ev = p.op("scalar", lambda e, c=c: e.activation(
                        tevac[:, c * T:(c + 1) * T], ov(c), AF.Identity,
                        bias=c_bs2h[:, c:c + 1], scale=0.5),
                        waits=[acc1] if c == 0 else [])
                dbg = [ev, p.mark("vector")]
                p.dma("sync", lambda e: e.dma_start(
                    out=dst_hbm.ap()[4], in_=tevac[:]), waits=dbg)
                p.dma("gpsimd", lambda e: e.dma_start(
                    out=dst_hbm.ap()[3][:, 0:1024], in_=h_g[:]), waits=dbg)
                p.dma("gpsimd", lambda e: e.dma_start(
                    out=dst_hbm.ap()[3][:, 1024:2048], in_=h_s[:]), waits=dbg)
                p.dma("sync", lambda e: e.dma_start(
                    out=dst_hbm.ap()[6], in_=AFA[:, 0:2048]), waits=dbg)
                for a in range(4):
                    p.dma("gpsimd", lambda e, a=a: e.dma_start(
                        out=dst_hbm.ap()[8 + a],
                        in_=wbuf1[:, a * 2048:(a + 1) * 2048]), waits=dbg)
                p._wait("sync", p.mark("dsync"))
                p._wait("sync", p.mark("dgp"))
                return None, None, None, None
            for ee in range(1, E):
                h1_mm[ee] = emit_ffn1(ee)
                emit_tail(ee - 1)
            emit_tail(E - 1)
            acc = ffn2_done[E - 1]
            for c in range(NCH):
                ev = p.op("scalar", lambda e, c=c: e.activation(
                    tevac[:, c * T:(c + 1) * T], ov(c), AF.Identity,
                    bias=c_bs2h[:, c:c + 1], scale=0.5),
                    waits=[acc] if c == 0 else [])
            ev_prev = ev
            ad = p.op("vector", lambda e, i=i: e.tensor_add(xa(i)[:], xa(i)[:], tevac[:]),
                      waits=[ldx, ev])
            st_c[i] = p.dma("sync", lambda e, i=i: e.dma_start(
                out=fm_tile(dst_hbm, i), in_=xa(i)[:]), waits=[ad])
            xln_use[i] = h1_mm[E - 1]
            if parts == "ABC1" and i == 0:
                dbg = [p.mark("vector"), p.mark("scalar"), st_c[0]]
                p.dma("gpsimd", lambda e: e.dma_start(
                    out=dst_hbm.ap()[1], in_=gb_sb[:]), waits=dbg)
                p.dma("gpsimd", lambda e: e.dma_start(
                    out=dst_hbm.ap()[2][:, 0:1024], in_=s1_sb[:]), waits=dbg)
                p.dma("gpsimd", lambda e: e.dma_start(
                    out=dst_hbm.ap()[3][:, 0:1024], in_=h_g[:]), waits=dbg)
                p.dma("gpsimd", lambda e: e.dma_start(
                    out=dst_hbm.ap()[3][:, 1024:2048], in_=h_s[:]), waits=dbg)
                p.dma("sync", lambda e: e.dma_start(
                    out=dst_hbm.ap()[4], in_=tevac[:]), waits=dbg)
                p.dma("gpsimd", lambda e: e.dma_start(
                    out=dst_hbm.ap()[5][:, 0:2048], in_=xlnb(0)[:]), waits=dbg)
        return st_c[NTC - 1], p.mark("vector"), p.mark("scalar"), p.mark("tensor")


    def attn_pass(src_hbm, dst_hbm, first_waits, attn_ready, aparts="ABCD"):
        bv_b = AFA[:, 6144:7168]
        ld_bvr = p.dma("sync", lambda e: e.dma_start(
            out=AFA[0:1, 4352:4352 + D], in_=dram["bv_row"][:]), waits=list(first_waits))
        mbv0 = p.op("tensor", lambda e: e.matmul(
            pb[0][:, 0:512], c_ones_r128[:], AFA[0:1, 4352:4864], start=True,
            stop=True, skip_group_check=True),
            waits=[ld_bvr] + list(first_waits), inc=True)
        mbv1 = p.op("tensor", lambda e: e.matmul(
            pb[1][:, 0:512], c_ones_r128[:], AFA[0:1, 4864:5376], start=True,
            stop=True, skip_group_check=True), inc=True)
        p.op("vector", lambda e: e.tensor_copy(bv_b[:, 0:512], pb[0][:, 0:512]),
             waits=[mbv0])
        bv_done = p.op("vector", lambda e: e.tensor_copy(bv_b[:, 512:D],
                                                         pb[1][:, 0:512]),
                       waits=[mbv1])

        hist = {k: {} for k in ("mm_s", "mm_q", "e1", "e2", "bm", "br",
                                "tt1", "tt2", "rr", "ld")}
        hist["tt2"][-1] = bv_done
        hist["tt1"][-1] = bv_done
        qsl = [ABF[:, 4096 + k * 256:4096 + (k + 1) * 256] for k in range(4)]
        ksl = [ABF[:, 5120 + k * 256:5120 + (k + 1) * 256] for k in range(4)]
        vsl = [ABF[:, 6144 + k * 1024:6144 + (k + 1) * 1024] for k in range(2)]
        qst_h, kst_h, vst_h, qev_h, kev_h, vev_h = {}, {}, {}, {}, {}, {}
        vmm_h = {}
        last_qkv_mm = {}
        def attn_head(k):
            if k < NT:
                xqd = ([qst_h.get((k - 2) * NCH + 7), kst_h.get((k - 2) * NCH + 7)]
                       if k % 2 == 0 else [vst_h.get(2 * (k - 2) + 1)])
                emit_ln_head(k, src_hbm, hist, list(first_waits) + [bv_done],
                             hist["tt1"].get(k - 2),
                             xb_deps=[last_qkv_mm.get(k - 2)],
                             xq_deps=xqd)
        for k in range(NT + 2):
            if k < 2:
                attn_head(k)
                continue
            i = k - 2
            tt2 = emit_ln_tail(
                i, hist, last_qkv_mm.get(i - 2),
                lambda i: [vev_h.get((2 * i - 1, 0))],
                lambda i: [vev_h.get((2 * i - 1, 1))])
            for W, bias, scale, dsthbm, sb_slots, ev_h, st_hh, c0buf, pbase in (
                ("q", c_bq8, 0.125, qs_hbm, qsl, qev_h, qst_h, c_q0, 0),
                ("k", c_bk, 1.0, ks_hbm, ksl, kev_h, kst_h, c_k0, 2),
            ):
                for c in range(NCH):
                    key = i * NCH + c
                    slot = key % 4
                    ps = pb[pbase + slot % 2][:, (slot // 2) * T:(slot // 2) * T + T]
                    for kc in range(NCH):
                        mm = p.op("tensor", lambda e, c=c, kc=kc, i=i, ps=ps, W=W:
                                  e.matmul(
                            ps, (Wqv if W == "q" else Wkv)(kc, c),
                            xlnb(i)[:, kc * T:(kc + 1) * T],
                            start=(kc == 0), stop=(kc == NCH - 1),
                            skip_group_check=True),
                            waits=([tt2, attn_ready, ev_h.get(key - 2)])
                            if kc == 0 else [],
                            inc=(kc == NCH - 1))
                    sbuf = sb_slots[slot]
                    ev_h[key] = p.op("scalar", lambda e, sbuf=sbuf, ps=ps, bias=bias,
                                     scale=scale, c=c: e.activation(
                        sbuf, ps, AF.Identity, bias=bias[:, c:c + 1], scale=scale),
                        waits=[mm, st_hh.get(key - 4)])
                    st_hh[key] = p.dma("sync", lambda e, sbuf=sbuf, c=c, i=i,
                                       dsthbm=dsthbm: e.dma_start(
                        out=dsthbm.ap()[c * 128:(c + 1) * 128, i * T:(i + 1) * T],
                        in_=sbuf), waits=[ev_h[key]])
                    if i == 0:
                        p.op("vector", lambda e, sbuf=sbuf, c=c, c0buf=c0buf:
                             e.tensor_copy(c0buf[:, c:c + 1], sbuf[:, 0:1]),
                             waits=[ev_h[key]])
            for tc in range(T // 128):
                tk = i * 2 + tc
                vbuf = vsl[tk % 2]
                for half in range(2):
                    ps = pb[4 + half]
                    for kc in range(NCH):
                        mm = p.op("tensor", lambda e, tc=tc, half=half, kc=kc, i=i,
                                  ps=ps: e.matmul(
                            ps[:, 0:512],
                            xlnb(i)[:, kc * T + tc * 128:kc * T + tc * 128 + 128],
                            Wvv(kc, half),
                            start=(kc == 0), stop=(kc == NCH - 1),
                            skip_group_check=True),
                            waits=([tt2] + [vev_h.get((tk - 1, half))])
                            if kc == 0 else [],
                            inc=(kc == NCH - 1))
                    vmm_h[(tk, half)] = mm
                    vev_h[(tk, half)] = p.op("vector", lambda e, vbuf=vbuf, half=half,
                                             ps=ps: e.tensor_tensor(
                        vbuf[:, half * 512:half * 512 + 512], ps[:, 0:512],
                        bv_b[:, half * 512:half * 512 + 512], ALU.add),
                        waits=[mm, vst_h.get(tk - 2)])
                vst_h[tk] = p.dma("sync", lambda e, i=i, tc=tc, vbuf=vbuf: e.dma_start(
                    out=vs_hbm.ap()[i * T + tc * 128:i * T + tc * 128 + 128, :],
                    in_=vbuf), waits=[vev_h[(tk, 1)]])
            last_qkv_mm[i] = vmm_h[(i * 2 + 1, 1)]
            attn_head(k)
        qkv_done = [p.mark("dsync"), p.mark("vector"), p.mark("scalar"),
                    p.mark("tensor")]
        q0bd = ABF[:, 8704:8832]
        ms = p.op("vector", lambda e: e.memset(q0bd, 0.0), waits=qkv_done[1:2])
        for c in range(NCH):
            p.op("vector", lambda e, c=c: e.tensor_copy(
                q0bd[0:64, c * 16 + 2 * c:c * 16 + 2 * c + 1], c_q0[0:64, c:c + 1]))
            p.op("vector", lambda e, c=c: e.tensor_copy(
                q0bd[64:128, c * 16 + 2 * c + 1:c * 16 + 2 * c + 2],
                c_q0[64:128, c:c + 1]))
        q0bd_done = p.mark("vector")
        gw_sb = AFA[0:16, 0:S]
        kblk = [wbuf1[:, 0:4096], wbuf1[:, 4096:8192]]
        gev_h, gld_h = {}, {}
        for sb in range(NCH):
            ldk = p.dma("sync", lambda e, sb=sb: e.dma_start(
                out=kblk[sb % 2][:].rearrange("p (c s) -> p c s", c=NCH),
                in_=ks_hbm.ap().rearrange("(c p) s -> p c s", p=128)
                    [:, :, sb * 512:(sb + 1) * 512]),
                waits=qkv_done[0:1] + ([gev_h.get(sb - 2)] if sb >= 2 else
                                       [p.mark("tensor")]))
            gld_h[sb] = ldk
            for c in range(NCH):
                mm = p.op("tensor", lambda e, sb=sb, c=c: e.matmul(
                    pb[0][0:16, 0:512], q0bd[:, c * 16:(c + 1) * 16],
                    kblk[sb % 2][:].rearrange("p (c s) -> p c s", c=NCH)[:, c, :],
                    start=(c == 0), stop=(c == NCH - 1), skip_group_check=True),
                    waits=([ldk, q0bd_done] + ([gev_h.get(sb - 1)] if sb >= 1 else []))
                    if c == 0 else [],
                    inc=(c == NCH - 1))
            gev_h[sb] = p.op("scalar", lambda e, sb=sb: e.activation(
                gw_sb[:, sb * 512:(sb + 1) * 512], pb[0][0:16, 0:512], AF.Copy),
                waits=[mm])
        negmax = AFA[0:16, 4096:4097]
        gsum = AFA[0:16, 4097:4098]
        ginv = AFA[0:16, 4098:4099]
        p.op("vector", lambda e: e.tensor_reduce(negmax, gw_sb, AX.X, ALU.max,
                                                 negate=True),
             waits=[gev_h[NCH - 1]])
        ex = p.op("scalar", lambda e: e.activation(gw_sb, gw_sb, AF.Exp, bias=negmax,
                                                   accum_out=gsum),
                  waits=[p.mark("vector")])
        p.op("vector", lambda e: e.reciprocal(ginv, gsum), waits=[ex])
        gpT = ABF[:, 0:512]
        tr_prev = None
        for j in range(32):
            tr = p.op("tensor", lambda e, j=j: e.transpose(
                pb[1][:, 0:16], gw_sb[:, j * 128:(j + 1) * 128], c_id[0:16, 0:16]),
                waits=[ex, tr_prev] if j == 0 else [tr_prev], inc=True)
            tr_prev = p.op("vector", lambda e, j=j: e.tensor_copy(
                gpT[:, j * 16:(j + 1) * 16], pb[1][:, 0:16]), waits=[tr])
        vblk = [wbuf1[:, 8192:9216], wbuf1[:, 9216:10240]]
        pvld_h, pvmm_h = {}, {}
        for j in range(32):
            ldv = p.dma("sync", lambda e, j=j: e.dma_start(
                out=vblk[j % 2][:], in_=vs_hbm.ap()[j * 128:(j + 1) * 128, :]),
                waits=[pvmm_h[j - 2]] if j >= 2 else [p.mark("tensor")])
            for half in range(2):
                mm = p.op("tensor", lambda e, j=j, half=half: e.matmul(
                    pb[2 + half][0:16, 0:512], gpT[:, j * 16:(j + 1) * 16],
                    vblk[j % 2][:, half * 512:half * 512 + 512],
                    start=(j == 0), stop=(j == 31), skip_group_check=True),
                    waits=[ldv, tr_prev] if half == 0 else [],
                    inc=(half == 1))
            pvmm_h[j] = mm
        og = AFA[0:16, 4608:5632]
        m1 = p.op("vector", lambda e: e.tensor_scalar_mul(
            og[:, 0:512], pb[2][0:16, 0:512], ginv), waits=[pvmm_h[31]])
        m2 = p.op("vector", lambda e: e.tensor_scalar_mul(
            og[:, 512:1024], pb[3][0:16, 0:512], ginv))
        msk16 = AFA[0:16, 5632:6656]
        ldm = p.dma("sync", lambda e: e.dma_start(out=msk16, in_=dram["mask16"][:]),
                    waits=qkv_done[1:2])
        mm1 = p.op("vector", lambda e: e.tensor_mul(og, og, msk16), waits=[ldm, m2])
        gctx = AFA[:, 6656:6664]
        tr_prev2 = mm1
        for c in range(NCH):
            tr = p.op("tensor", lambda e, c=c: e.transpose(
                pb[4][:, 0:16], og[:, c * 128:(c + 1) * 128], c_id[0:16, 0:16]),
                waits=[tr_prev2], inc=True)
            cp = p.op("vector", lambda e, c=c: e.tensor_copy(
                AFA[:, 6664:6680], pb[4][:, 0:16]), waits=[tr])
            tr_prev2 = p.op("vector", lambda e, c=c: e.tensor_reduce(
                gctx[:, c:c + 1], AFA[:, 6664:6680], AX.X, ALU.add))
        gctx_b = ABF[:, 8832:8840]
        gcb = p.op("vector", lambda e: e.tensor_copy(gctx_b, gctx))
        ldx0 = p.dma("sync", lambda e: e.dma_start(out=xa(0)[:], in_=fm_tile(src_hbm, 0)),
                     waits=[p.mark("vector")])
        for c in range(NCH):
            for kc in range(NCH):
                mm = p.op("tensor", lambda e, c=c, kc=kc: e.matmul(
                    pb[5][:, c:c + 1], Wov(kc, c), gctx_b[:, kc:kc + 1],
                    start=(kc == 0 and c == 0), stop=(c == NCH - 1 and kc == NCH - 1),
                    skip_group_check=True),
                    waits=[gcb] if (c == 0 and kc == 0) else [],
                    inc=(c == NCH - 1 and kc == NCH - 1))
        for c in range(NCH):
            t0e = p.op("scalar", lambda e, c=c: e.activation(
                AFA[:, 6680 + c:6681 + c], pb[5][:, c:c + 1], AF.Identity,
                bias=c_bo[:, c:c + 1]), waits=[mm] if c == 0 else [])
        t0a = p.op("vector", lambda e: e.tensor_tensor(
            AFA[:, 6688:6696].rearrange("p (c a) -> p c a", a=1),
            xa(0)[:].rearrange("p (c t) -> p c t", c=NCH)[:, :, 0:1],
            AFA[:, 6680:6688].rearrange("p (c a) -> p c a", a=1),
            ALU.add), waits=[ldx0, t0e])
        for c in range(NCH):
            p.dma("sync", lambda e, c=c: e.dma_start(
                out=dst_hbm.ap()[0][:, c * T:c * T + 1],
                in_=AFA[:, 6688 + c:6689 + c],
                allow_slow_non_contiguous=True), waits=[t0a])
        glob_done = [p.mark("dsync"), p.mark("vector"), p.mark("scalar"),
                     p.mark("tensor")]
        if aparts == "ABC":
            return glob_done

        qbuf = [wbuf1[:, 0:1024], wbuf1[:, 1024:2048]]
        kbuf = [wbuf1[:, 2048:5120], wbuf1[:, 5120:8192]]
        vbuf = [wbuf1[:, 8192:11264], wbuf1[:, 11264:14336]]
        ET = [wbuf1[:, 14336:17408], wbuf1[:, 17408:20480]]
        ctx_tok = wbuf1[:, 20480:21504]
        ctxg2 = [wbuf1[:, 21504:25600], wbuf1[:, 25600:29696]]
        rinvT = AFA[:, 6144:6176]
        tevc = AFA[:, 6176:6688]
        x1c = [AFA[:, 6688:7200], AFA[:, 7200:7712]]

        stq_h, pvw_h, trp_h = {}, {}, {}
        woev_h, womm_h, wost_h, ad_h = {}, {}, {}, {}
        rc_h, ld_h = {}, {}
        ETS = [ET[0][:, k * 768:(k + 1) * 768] for k in range(4)] + \
              [ET[1][:, k * 768:(k + 1) * 768] for k in range(4)]
        rslot_guard = [glob_done[2]] * 2
        ets_guard = [glob_done[3]] * 8
        cmb_prev = [glob_done[1]]
        trp_guard = None
        unitctr = [0]
        NJ = 31
        DLVL = 9
        if aparts.startswith("ABCD") and len(aparts) > 4:
            arg = aparts[4:]
            if arg.startswith("L"):
                DLVL = int(arg[1:])
                NJ = 1
            else:
                NJ = int(arg)

        def jparams(j):
            ks0 = 1 if j == 0 else 128 * (j - 1) + 1
            nkc = 2 if j in (0, 30) else 3
            return ks0, nkc, nkc * 128

        def emit_loads(jx):
            ks0, nkc, kw = jparams(jx)
            ldq = p.dma("sync", lambda e, jx=jx: e.dma_start(
                out=qbuf[jx % 2][:].rearrange("p (c u) -> p c u", c=NCH),
                in_=qs_hbm.ap().rearrange("(c p) s -> p c s", p=128)
                    [:, :, 128 * jx + 1:128 * jx + 129]),
                waits=(glob_done[0:1] if jx < 2 else []) +
                      ([stq_h[jx - 2]] if jx >= 2 else glob_done[3:4]))
            ldk = p.dma("sync", lambda e, jx=jx, ks0=ks0, kw=kw: e.dma_start(
                out=kbuf[jx % 2][:, 0:NCH * kw].rearrange("p (c u) -> p c u", c=NCH),
                in_=ks_hbm.ap().rearrange("(c p) s -> p c s", p=128)
                    [:, :, ks0:ks0 + kw]),
                waits=[stq_h[jx - 2]] if jx >= 2 else glob_done[3:4])
            ldv = p.dma("sync", lambda e, jx=jx, ks0=ks0, kw=kw: e.dma_start(
                out=vbuf[jx % 2][:, 0:NCH * kw].rearrange("p (b d) -> p b d", d=D),
                in_=vs_hbm.ap()[ks0:ks0 + kw, :].rearrange("(b p) d -> p b d", p=128)),
                waits=[pvw_h[jx - 2]] if jx >= 2 else glob_done[3:4])
            ld_h[jx] = (ldq, ldk, ldv)

        def emit_wo(g, jlast):
            ctxg = ctxg2[g % 2]
            if jlast == 30:
                mz = p.op("vector", lambda e, ctxg=ctxg: e.memset(
                    ctxg[:].rearrange("p (c u) -> p c u", c=NCH)[:, :, 384:512],
                    0.0), waits=[trp_h[jlast]])
                zdep = [mz]
            else:
                zdep = []
            wo_w = 511 if g == 7 else 512
            for c in range(NCH):
                wpar = unitctr[0] % 2
                unitctr[0] += 1
                woff = wpar * 1024
                for kc in range(NCH):
                    mm = p.op("tensor", lambda e, c=c, kc=kc, woff=woff,
                              ctxg=ctxg: e.matmul(
                        PB[:, woff:woff + 512], Wov(kc, c),
                        ctxg[:, kc * 512:(kc + 1) * 512],
                        start=(kc == 0), stop=(kc == NCH - 1),
                        skip_group_check=True),
                        waits=([trp_h[jlast], rslot_guard[wpar]] + zdep)
                        if kc == 0 else [],
                        inc=(kc == NCH - 1))
                womm_h[g] = mm
                ev = p.op("scalar", lambda e, c=c, woff=woff: e.activation(
                    tevc, PB[:, woff:woff + 512], AF.Identity,
                    bias=c_bo[:, c:c + 1]),
                    waits=[mm, p.mark("vector")] if c == 0 else
                          [mm, ad_h[(g, c - 1)]])
                woev_h[(g, c)] = ev
                rslot_guard[wpar] = ev
                pw = []
                pw.append(p.dma("sync", lambda e, g=g, c=c: e.dma_start(
                    out=x1c[c % 2][:, 0:255],
                    in_=src_hbm.ap()[2 * g][:, c * T + 1:(c + 1) * T]),
                    waits=[wost_h.get((g, c - 2))]))
                pw.append(p.dma("sync", lambda e, g=g, c=c: e.dma_start(
                    out=x1c[c % 2][:, 255:511],
                    in_=src_hbm.ap()[2 * g + 1][:, c * T:(c + 1) * T])))
                if g < 7:
                    pw.append(p.dma("sync", lambda e, g=g, c=c: e.dma_start(
                        out=x1c[c % 2][:, 511:512],
                        in_=src_hbm.ap()[2 * g + 2][:, c * T:c * T + 1],
                        allow_slow_non_contiguous=True)))
                ad = p.op("vector", lambda e, c=c, wo_w=wo_w: e.tensor_add(
                    x1c[c % 2][:, 0:wo_w], x1c[c % 2][:, 0:wo_w],
                    tevc[:, 0:wo_w]), waits=pw + [ev])
                ad_h[(g, c)] = ad
                s1_ = p.dma("sync", lambda e, g=g, c=c: e.dma_start(
                    out=dst_hbm.ap()[2 * g][:, c * T + 1:(c + 1) * T],
                    in_=x1c[c % 2][:, 0:255]), waits=[ad])
                s2_ = p.dma("sync", lambda e, g=g, c=c: e.dma_start(
                    out=dst_hbm.ap()[2 * g + 1][:, c * T:(c + 1) * T],
                    in_=x1c[c % 2][:, 255:511]))
                if g < 7:
                    s3_ = p.dma("sync", lambda e, g=g, c=c: e.dma_start(
                        out=dst_hbm.ap()[2 * g + 2][:, c * T:c * T + 1],
                        in_=x1c[c % 2][:, 511:512],
                        allow_slow_non_contiguous=True))
                    wost_h[(g, c)] = s3_
                else:
                    wost_h[(g, c)] = s2_

        emit_loads(0)
        if NJ > 1:
            emit_loads(1)
        pending_wo = None
        for j in range(NJ):
            g, jj = j // 4, j % 4
            ks0, nkc, kw = jparams(j)
            has_A = j >= 1
            has_B = j <= 29
            kcA = (0, 1)
            kcB = (0, 1) if j == 0 else (1, 2)
            cscale = 1.0 if j in (0, 30) else 0.5
            ldq, ldk, ldv = ld_h[j]
            kb3 = kbuf[j % 2][:, 0:NCH * kw].rearrange("p (c u) -> p c u", c=NCH)
            qb3 = qbuf[j % 2][:].rearrange("p (c u) -> p c u", c=NCH)
            vb3 = vbuf[j % 2][:, 0:NCH * kw].rearrange("p (b d) -> p b d", d=D)
            if DLVL < 1:
                stq_h[j] = p.mark("tensor")
                continue
            ust = [None] * 8
            for u2 in range(9):
                if u2 < 8:
                    upar = unitctr[0] % 2
                    unitctr[0] += 1
                    uc = j * 8 + u2
                    roff = upar * 1024
                    ets = ETS[uc % 8]
                    for l in range(2):
                        h = 2 * u2 + l
                        po = 64 * (h % 2)
                        for kc in range(nkc):
                            w = []
                            if l == 0 and kc == 0:
                                w = [ldq, ldk, rslot_guard[upar]]
                                if j == 0 and u2 == 0:
                                    w += [glob_done[1], glob_done[2]]
                            mm = p.op("tensor", lambda e, h=h, l=l, kc=kc, po=po,
                                      roff=roff, kb3=kb3, qb3=qb3: e.matmul(
                                PB[:, roff + l * 512 + kc * 128:
                                   roff + l * 512 + kc * 128 + 128],
                                kb3[po:po + 64, h // 2, kc * 128:(kc + 1) * 128],
                                qb3[po:po + 64, h // 2, :],
                                start=(kc == 0), stop=(kc == nkc - 1),
                                skip_group_check=True),
                                waits=w, inc=(l == 1 and kc == nkc - 1))
                    st_mm = mm
                    if DLVL == 1:
                        rslot_guard[upar] = st_mm
                        continue
                    pview = PB[:, roff:roff + 1024].rearrange(
                        "p (l x) -> p l x", l=2)[:, :, 0:kw]
                    etv = ets[:, 0:2 * kw].rearrange("p (l x) -> p l x", l=2)
                    ex = p.op("scalar", lambda e, pview=pview, etv=etv:
                              e.activation(etv, pview, AF.Exp),
                              waits=[st_mm, ets_guard[uc % 8]])
                    if DLVL < 3:
                        rslot_guard[upar] = ex
                        continue
                    ust[u2] = (upar, roff, ets, ex, uc)
                if u2 >= 1 and DLVL >= 3 and ust[u2 - 1] is not None:
                    v2 = u2 - 1
                    vpar, roff, ets, ex, uc = ust[v2]
                    first = True
                    for l in range(2):
                        for si, (hs, kcp) in enumerate(((has_A, kcA),
                                                        (has_B, kcB))):
                            if not hs:
                                continue
                            for z, kc in enumerate(kcp):
                                w = [ex] if first else []
                                first = False
                                mm = p.op("tensor", lambda e, l=l, kc=kc, si=si,
                                          z=z, ets=ets, kw=kw, roff=roff:
                                          e.matmul(
                                    PB[:, roff + 896 + l * 2 + si:
                                       roff + 897 + l * 2 + si],
                                    ets[:, l * kw + kc * 128:
                                        l * kw + (kc + 1) * 128],
                                    c_ones_c[:],
                                    start=(z == 0), stop=(z == 1),
                                    skip_group_check=True),
                                    waits=w, inc=(l == 1 and z == 1))
                    sums_mm = mm
                    rc = p.op("vector", lambda e, v2=v2, roff=roff: e.reciprocal(
                        rinvT[:, v2 * 4:(v2 + 1) * 4],
                        PB[:, roff + 896:roff + 900]),
                        waits=[sums_mm])
                    if cscale != 1.0:
                        p.op("vector", lambda e, v2=v2, cscale=cscale:
                             e.tensor_scalar_mul(rinvT[:, v2 * 4:(v2 + 1) * 4],
                                                 rinvT[:, v2 * 4:(v2 + 1) * 4],
                                                 cscale))
                    rc_h[uc] = rc
                    rslot_guard[vpar] = rc
                    if DLVL < 4:
                        continue
                    for si, (hs, kcp) in enumerate(((has_A, kcA), (has_B, kcB))):
                        if not hs:
                            continue
                        coff = 2048 if si == 0 else 3072
                        for l in range(2):
                            h = 2 * v2 + l
                            for z, kc in enumerate(kcp):
                                w = []
                                if l == 0 and z == 0 and v2 == 0 and                                         si == (0 if has_A else 1):
                                    w = [ldv, cmb_prev[0]]
                                mm = p.op("tensor", lambda e, l=l, h=h, kc=kc,
                                          coff=coff, v2=v2, ets=ets, vb3=vb3,
                                          kw=kw, z=z: e.matmul(
                                    PB[:, coff + v2 * 128 + l * 64:
                                       coff + v2 * 128 + (l + 1) * 64],
                                    ets[:, l * kw + kc * 128:
                                        l * kw + (kc + 1) * 128],
                                    vb3[:, kc, h * 64:(h + 1) * 64],
                                    start=(z == 0 and l == 0 and v2 % 4 == 0),
                                    stop=(z == 1), skip_group_check=True),
                                    waits=w, inc=(l == 1 and z == 1))
                    pvw_h[j] = mm
                    ets_guard[uc % 8] = mm
            if DLVL < 4:
                stq_h[j] = st_mm
                continue
            stq_h[j] = st_mm
            if has_A and has_B:
                for si, coff, tb in ((0, 2048, 4096), (1, 3072, 5120)):
                    rv = bass.AP(rinvT.tensor, rinvT.offset + si,
                                 [rinvT.ap[0], [2, 16], [0, 64]])
                    pv_ = PB[:, coff:coff + 1024].rearrange(
                        "p (h d) -> p h d", h=16)
                    t1v = AFA[:, tb:tb + 1024].rearrange("p (h d) -> p h d", h=16)
                    p.op("vector", lambda e, pv_=pv_, rv=rv, t1v=t1v:
                         e.tensor_tensor(t1v, pv_, rv, ALU.mult),
                         waits=[pvw_h[j], trp_guard] if si == 0 else [])
                cmb = p.op("vector", lambda e: e.tensor_add(
                    ctx_tok[:], AFA[:, 4096:5120], AFA[:, 5120:6144]))
            else:
                si = 0 if has_A else 1
                coff = 2048 if has_A else 3072
                rv = bass.AP(rinvT.tensor, rinvT.offset + si,
                             [rinvT.ap[0], [2, 16], [0, 64]])
                pv_ = PB[:, coff:coff + 1024].rearrange("p (h d) -> p h d", h=16)
                ctv = ctx_tok[:].rearrange("p (h d) -> p h d", h=16)
                cmb = p.op("vector", lambda e, pv_=pv_, rv=rv, ctv=ctv:
                           e.tensor_tensor(ctv, pv_, rv, ALU.mult),
                           waits=[pvw_h[j], trp_guard])
            cmb_prev[0] = cmb
            if j + 2 < NJ:
                emit_loads(j + 2)
            for c in range(NCH):
                trp = p.dma("sync", lambda e, c=c, jj=jj, g=g: e.dma_start(
                    out=ctxg2[g % 2][:, c * 512 + jj * 128:
                                     c * 512 + jj * 128 + 128],
                    in_=ctx_tok[:, c * 128:(c + 1) * 128], transpose=True),
                    waits=([cmb, womm_h.get(g - 2)] if c == 0 else []))
            trp_h[j] = trp
            trp_guard = trp
            if pending_wo is not None:
                emit_wo(*pending_wo)
                pending_wo = None
            if jj == 3 or j == 30:
                pending_wo = (g, j)
        if pending_wo is not None and DLVL >= 4:
            emit_wo(*pending_wo)
        return [p.mark("dsync"), p.mark("vector"), p.mark("scalar"),
                p.mark("tensor"), p.mark("dscalar")]
    p1_dst = out_fm if stop_after == 1 else x1_hbm
    p1_done = ffn_pass(x_in, p1_dst, [p0_done_v, ld_bs2], parts=parts)
    if stop_after >= 2:
        attn_ready = stream_attn(waits=[p1_done[3]])
        p2_dst = out_fm if stop_after == 2 else x2_hbm
        p2_done = attn_pass(x1_hbm, p2_dst,
                            [p1_done[0], p1_done[1], p1_done[2]],
                            attn_ready, aparts=aparts)
    if stop_after >= 3:
        w1b = stream_w1(waits=[p2_done[3], p2_done[4], p2_done[1]])
        w2b = stream_w2(waits=[p2_done[3]])
        p3_done = ffn_pass(x2_hbm, out_fm,
                           [p2_done[0], p2_done[1], p2_done[2], p2_done[4]],
                           wdeps=(w1b, w2b, ws_ready))

    p._wait("sync", ("dsync", p.cnt["dsync"]))
    p._wait("sync", ("dgp", p.cnt["dgp"]))
    p.run_block()
    return nc, p



def _stage_inputs(inputs):
    f32 = np.float32
    x = np.asarray(inputs["L0_attn_hidden_states"], f32)

    def pm(v):
        return np.ascontiguousarray(np.asarray(v, f32).reshape(-1, 128).T)

    onehot8 = np.zeros((E, E * 128), ml_dtypes.bfloat16)
    for e in range(E):
        onehot8[e, e * 128:(e + 1) * 128] = 1
    mask16 = np.zeros((H, D), f32)
    for h in range(H):
        mask16[h, h * DH:(h + 1) * DH] = 1
    b1 = np.asarray(inputs["L0_ffn_b1"], f32)
    gate_W = np.asarray(inputs["L0_ffn_gate_W"], f32)

    shared = {
        "Wq": np.asarray(inputs["L0_attn_Wq"], f32),
        "Wk": np.asarray(inputs["L0_attn_Wk"], f32),
        "Wv": np.asarray(inputs["L0_attn_Wv"], f32),
        "Wo": np.asarray(inputs["L0_attn_Wo"], f32),
        "bq": pm(inputs["L0_attn_bq"]), "bk": pm(inputs["L0_attn_bk"]),
        "bo": pm(inputs["L0_attn_bo"]),
        "bv_row": np.asarray(inputs["L0_attn_bv"], f32).reshape(1, D),
        "gateW": np.ascontiguousarray(
            gate_W.reshape(NCH, 128, E).transpose(1, 0, 2).reshape(128, NCH * E)),
        "W1": np.asarray(inputs["L0_ffn_W1"], f32),
        "W2": np.asarray(inputs["L0_ffn_W2"], f32),
        "b1": np.ascontiguousarray(
            b1.reshape(E, 4, 128).transpose(2, 0, 1).reshape(128, E * 4)),
        "b2": np.asarray(inputs["L0_ffn_b2"], f32),
        "Ws1": np.asarray(inputs["L0_ffn_Ws1"], f32),
        "Ws2": np.asarray(inputs["L0_ffn_Ws2"], f32),
        "bs1": pm(inputs["L0_ffn_bs1"]), "bs2": pm(inputs["L0_ffn_bs2"]),
        "rs": np.asarray(inputs["L0_ffn_route_scale"], f32).reshape(1, 1),
        "ones_r128_f32": np.ones((1, 128), f32),
        "ones_r128_bf16": np.ones((1, 128), ml_dtypes.bfloat16),
        "ones_r8_f32": np.ones((1, 8), f32),
        "ones_c_bf16": np.ones((128, 1), ml_dtypes.bfloat16),
        "onehot8": onehot8,
        "id_f32": np.eye(128, dtype=f32),
        "mask16": mask16,
    }
    in_maps = []
    for b in range(8):
        xf = x[b].T
        xt = np.ascontiguousarray(
            xf.reshape(NCH, 128, NT, T).transpose(2, 1, 0, 3).reshape(NT, 128, NCH * T))
        m = dict(shared)
        m["x_fm"] = xt
        in_maps.append(m)
    return in_maps


def _untile(out_t):
    a = np.asarray(out_t, np.float32).reshape(NT, 128, NCH, T)
    return a.transpose(2, 1, 0, 3).reshape(D, S).T


def kernel(**inputs):
    in_maps = _stage_inputs(inputs)
    nc, _ = build_nc(stop_after=3)
    res = run_bass_kernel_spmd(nc, in_maps, core_ids=list(range(8)))
    return np.stack([_untile(res.results[b]["out_fm"]) for b in range(8)])


# version 14
# speedup vs baseline: 1.0065x; 1.0065x over previous
import numpy as np
import ml_dtypes

import concourse.bass as bass
import concourse.mybir as mybir
from concourse.bass_utils import run_bass_kernel_spmd
from concourse.bass import broadcast_tensor_aps

F32 = mybir.dt.float32
BF16 = mybir.dt.bfloat16
AF = mybir.ActivationFunctionType
ALU = mybir.AluOpType
AX = mybir.AxisListType

D = 1024
S = 4096
NCH = 8
H = 16
DH = 64
E = 8
HID = 512
T = 256
NT = S // T
NWIN = 30
LC = S - 1

ENGS = ("sync", "tensor", "vector", "scalar", "gpsimd")


class Prog:

    def __init__(self, nc):
        self.nc = nc
        self.q = {e: [] for e in ENGS}
        self.sems = {}
        self.cnt = {}
        for name in ("tensor", "vector", "scalar", "gpsimd", "dsync", "dscalar", "dgp"):
            self.sems[name] = nc.alloc_semaphore("sem_" + name)
            self.cnt[name] = 0

    def _wait(self, eng, ms):
        sn, v = ms
        if v <= 0:
            return
        assert v <= self.cnt[sn], f"{eng} waits {sn}>={v}, emitted {self.cnt[sn]}"
        s = self.sems[sn]
        self.q[eng].append(lambda e, s=s, v=v: e.wait_ge(s, v))

    def op(self, eng, fn, waits=(), inc=None):
        serial = eng in ("vector", "scalar", "gpsimd")
        if serial:
            self._wait(eng, (eng, self.cnt[eng]))
        for ms in waits:
            if ms is not None:
                self._wait(eng, ms)
        if serial or inc:
            s = self.sems[eng]
            self.cnt[eng] += 1
            n = self.cnt[eng]
            self.q[eng].append(lambda e, s=s, fn=fn: fn(e).then_inc(s, 1))
            return (eng, n)
        self.q[eng].append(fn)
        return (eng, 0)

    def dma(self, eng, fn, waits=()):
        dname = {"sync": "dsync", "scalar": "dscalar", "gpsimd": "dgp"}[eng]
        if eng in ("scalar", "gpsimd"):
            self._wait(eng, (eng, self.cnt[eng]))
        for ms in waits:
            if ms is not None:
                self._wait(eng, ms)
        s = self.sems[dname]
        self.cnt[dname] += 16
        n = self.cnt[dname]
        self.q[eng].append(lambda e, s=s, fn=fn: fn(e).then_inc(s, 16))
        return (dname, n)

    def mark(self, eng):
        return (eng, self.cnt[eng])

    def run_block(self):
        with self.nc.Block() as block:
            block.sync(lambda e: [f(e) for f in self.q["sync"]])
            block.tensor(lambda e: [f(e) for f in self.q["tensor"]])
            block.vector(lambda e: [f(e) for f in self.q["vector"]])
            block.scalar(lambda e: [f(e) for f in self.q["scalar"]])
            block.gpsimd(lambda e: [f(e) for f in self.q["gpsimd"]])


def bcast_mid(big3, small2):
    s3 = small2.rearrange("p (a t) -> p a t", a=1)
    return broadcast_tensor_aps(big3, s3)[1]


def build_nc(stop_after=3, parts="ABC", aparts="ABCD"):
    nc = bass.Bass("TRN2", target_bir_lowering=False, dynamic_dma_scratch_size=16384,
                   detect_race_conditions=False)
    dram = {}

    def din(name, shape, dt=F32):
        dram[name] = nc.dram_tensor(name, list(shape), dt, kind="ExternalInput")
        return dram[name]

    x_in = din("x_fm", (NT, 128, NCH * T))
    din("Wq", (D, D)); din("Wk", (D, D)); din("Wv", (D, D)); din("Wo", (D, D))
    din("bq", (128, NCH)); din("bk", (128, NCH)); din("bo", (128, NCH))
    din("bv_row", (1, D))
    din("gateW", (128, NCH * E))
    din("W1", (E, D, HID)); din("W2", (E, HID, D))
    din("b1", (128, E * 4)); din("b2", (E, D))
    din("Ws1", (D, HID)); din("Ws2", (HID, D))
    din("bs1", (128, 4)); din("bs2", (128, NCH))
    din("rs", (1, 1))
    din("ones_r128_f32", (1, 128))
    din("ones_r128_bf16", (1, 128), BF16)
    din("ones_r8_f32", (1, 8))
    din("ones_c_bf16", (128, 1), BF16)
    din("onehot8", (E, E * 128), BF16)
    din("id_f32", (128, 128))
    din("mask16", (H, D))

    out_fm = nc.dram_tensor("out_fm", [NT, 128, NCH * T], F32, kind="ExternalOutput")

    x1_hbm = nc.dram_tensor("x1_s", [NT, 128, NCH * T], F32, kind="Internal")
    x2_hbm = nc.dram_tensor("x2_s", [NT, 128, NCH * T], F32, kind="Internal")
    xln_hbm = nc.dram_tensor("xln_s", [NT, 128, NCH * T], BF16, kind="Internal")
    qs_hbm = nc.dram_tensor("qs_s", [D, S], BF16, kind="Internal")
    ks_hbm = nc.dram_tensor("ks_s", [D, S], BF16, kind="Internal")
    vs_hbm = nc.dram_tensor("vs_s", [S, D], BF16, kind="Internal")
    rows_hbm = nc.dram_tensor("rows_s", [1, S], F32, kind="Internal")
    nmh_hbm = nc.dram_tensor("nmh_s", [1, S], BF16, kind="Internal")

    wbuf1 = nc.alloc_sbuf_tensor("wbuf1", [128, E * NCH * HID], BF16)
    wbuf2 = nc.alloc_sbuf_tensor("wbuf2", [128, E * 4 * D], BF16)
    wbufS = nc.alloc_sbuf_tensor("wbufS", [128, NCH * HID + 4 * D], BF16)

    def W1v(e, kc, hc):
        base = (e * NCH + kc) * HID + hc * 128
        return wbuf1[:, base:base + 128]

    def W2v(e, kc, mc):
        base = (e * 4 + kc) * D + mc * 128
        return wbuf2[:, base:base + 128]

    def Ws1v(kc, hc):
        return wbufS[:, kc * HID + hc * 128:kc * HID + hc * 128 + 128]

    def Ws2v(kc, mc):
        base = NCH * HID + kc * D + mc * 128
        return wbufS[:, base:base + 128]

    def Wqv(kc, mc):
        return wbuf1[:, (0 * NCH + kc) * D + mc * 128:][:, 0:128]

    def Wkv(kc, mc):
        return wbuf1[:, (NCH + kc) * D + mc * 128:][:, 0:128]

    def Wvv(kc, half):
        return wbuf1[:, (2 * NCH + kc) * D + half * 512:][:, 0:512]

    def Wov(kc, mc):
        return wbuf2[:, kc * D + mc * 128:][:, 0:128]

    c_bq8 = nc.alloc_sbuf_tensor("c_bq8", [128, NCH], F32)
    c_bk = nc.alloc_sbuf_tensor("c_bk", [128, NCH], F32)
    c_bo = nc.alloc_sbuf_tensor("c_bo", [128, NCH], F32)
    c_bs2h = nc.alloc_sbuf_tensor("c_bs2h", [128, NCH], F32)
    c_b1 = nc.alloc_sbuf_tensor("c_b1", [128, E * 4], F32)
    c_bs1 = nc.alloc_sbuf_tensor("c_bs1", [128, 4], F32)
    c_b2bf = nc.alloc_sbuf_tensor("c_b2bf", [E, D], BF16)
    c_gw = nc.alloc_sbuf_tensor("c_gw", [128, NCH * E], BF16)
    c_rs = nc.alloc_sbuf_tensor("c_rs", [128, 1], F32)
    c_ones_r128 = nc.alloc_sbuf_tensor("c_ones_r128", [1, 128], F32)
    c_ones_r128b = nc.alloc_sbuf_tensor("c_ones_r128b", [1, 128], BF16)
    c_ones_c = nc.alloc_sbuf_tensor("c_ones_c", [128, 1], BF16)
    c_oh8 = nc.alloc_sbuf_tensor("c_oh8", [E, E * 128], BF16)
    c_id = nc.alloc_sbuf_tensor("c_id", [128, 128], F32)
    c_q0 = nc.alloc_sbuf_tensor("c_q0", [128, NCH], BF16)
    c_k0 = nc.alloc_sbuf_tensor("c_k0", [128, NCH], BF16)
    gbuf = nc.alloc_sbuf_tensor("gbuf", [E, S], BF16)

    AFA = nc.alloc_sbuf_tensor("AFA", [128, 8192], F32)
    ABF = nc.alloc_sbuf_tensor("ABF", [128, 9216], BF16)

    def xa(i):
        return AFA[:, (i % 2) * 2048:(i % 2) * 2048 + 2048]

    def xa3(i):
        return xa(i).rearrange("p (c t) -> p c t", c=NCH)

    tevac = AFA[:, 4096:6144]
    tevac3 = tevac.rearrange("p (c t) -> p c t", c=NCH)

    def xlnb(i):
        return ABF[:, (i % 2) * 2048:(i % 2) * 2048 + 2048]

    def xlnb3(i):
        return xlnb(i).rearrange("p (c t) -> p c t", c=NCH)

    h_g = ABF[:, 4096:5120]
    h_s = ABF[:, 5120:6144]
    s1_sb = ABF[:, 6144:7168]
    gb_sb = ABF[:, 7168:9216]

    PB = nc.alloc_psum_tensor("PB", [128, 4096], F32)
    pb = [PB[:, i * 512:(i + 1) * 512] for i in range(8)]

    def fm_tile(hbm, i):
        return hbm.ap()[i]

    p = Prog(nc)

    ld_bq = p.dma("sync", lambda e: e.dma_start(out=AFA[:, 64:64 + NCH], in_=dram["bq"][:]))
    ld_bs2 = p.dma("sync", lambda e: e.dma_start(out=AFA[:, 96:96 + NCH], in_=dram["bs2"][:]))
    for name, dst in [("bk", c_bk), ("bo", c_bo), ("b1", c_b1), ("bs1", c_bs1),
                      ("ones_r128_f32", c_ones_r128), ("ones_r128_bf16", c_ones_r128b),
                      ("id_f32", c_id),
                      ("ones_c_bf16", c_ones_c), ("onehot8", c_oh8)]:
        p.dma("sync", lambda e, d=dst, s=dram[name]: e.dma_start(out=d[:], in_=s[:]))
    ld_ones = p.mark("dsync")
    gw_ready = p.dma("gpsimd", lambda e: e.dma_start(out=c_gw[:], in_=dram["gateW"][:]))
    b2_ready = p.dma("gpsimd", lambda e: e.dma_start(out=c_b2bf[:], in_=dram["b2"][:]))
    ld_rs = p.dma("sync", lambda e: e.dma_start(out=AFA[0:1, 128:129], in_=dram["rs"][:]))
    ld_or8 = p.dma("sync", lambda e: e.dma_start(out=AFA[0:1, 136:144], in_=dram["ones_r8_f32"][:]))
    mrs = p.op("tensor", lambda e: e.matmul(pb[0][0:8, 0:1], AFA[0:1, 136:144],
                                            AFA[0:1, 128:129], start=True, stop=True),
               waits=[ld_rs, ld_or8], inc=True)
    p.op("vector", lambda e: e.tensor_copy(c_rs[0:8, 0:1], pb[0][0:8, 0:1]), waits=[mrs])
    p.op("vector", lambda e: e.tensor_scalar_mul(c_bq8[:], AFA[:, 64:64 + NCH], 0.125),
         waits=[ld_bq])
    p.op("vector", lambda e: e.tensor_scalar_mul(c_bs2h[:], AFA[:, 96:96 + NCH], 0.5),
         waits=[ld_bs2])
    p0_done_v = p.mark("vector")

    def stream_w1(waits=()):
        rl = []
        for ee in range(E):
            for half in range(2):
                src = dram["W1"].ap()[ee][half * 512:half * 512 + 512].rearrange(
                    "(kc p) h -> p kc h", p=128)
                base = ee * NCH * HID + half * 4 * HID
                dst = wbuf1[:, base:base + 4 * HID].rearrange(
                    "p (kc h) -> p kc h", kc=4)
                r = p.dma("gpsimd", lambda e, src=src, dst=dst: e.dma_start(
                    out=dst, in_=src), waits=waits if ee == 0 and half == 0 else [])
            rl.append(r)
        return rl

    def stream_w2(waits=()):
        rl = []
        for ee in range(E):
            src = dram["W2"].ap()[ee].rearrange("(kc p) d -> p kc d", p=128)
            dst = wbuf2[:, ee * 4 * D:(ee + 1) * 4 * D].rearrange(
                "p (kc d) -> p kc d", kc=4)
            rl.append(p.dma("gpsimd", lambda e, src=src, dst=dst: e.dma_start(
                out=dst, in_=src), waits=waits if ee == 0 else []))
        return rl

    def stream_ws(waits=()):
        for half in range(2):
            s1 = dram["Ws1"].ap()[half * 512:half * 512 + 512].rearrange(
                "(kc p) h -> p kc h", p=128)
            d1 = wbufS[:, half * 4 * HID:(half + 1) * 4 * HID].rearrange(
                "p (kc h) -> p kc h", kc=4)
            p.dma("gpsimd", lambda e, s1=s1, d1=d1: e.dma_start(out=d1, in_=s1),
                  waits=waits)
        s2 = dram["Ws2"].ap().rearrange("(kc p) d -> p kc d", p=128)
        d2 = wbufS[:, NCH * HID:].rearrange("p (kc d) -> p kc d", kc=4)
        return p.dma("gpsimd", lambda e: e.dma_start(out=d2, in_=s2), waits=waits)

    def stream_attn(waits=()):
        r = None
        for name, off, buf in [("Wq", 0, wbuf1), ("Wk", NCH * D, wbuf1),
                               ("Wv", 2 * NCH * D, wbuf1), ("Wo", 0, wbuf2)]:
            for half in range(2):
                src = dram[name].ap()[half * 512:half * 512 + 512].rearrange(
                    "(kc p) m -> p kc m", p=128)
                b2 = off + half * 4 * D
                r = p.dma("gpsimd", lambda e, b2=b2, buf=buf, src=src: e.dma_start(
                    out=buf[:, b2:b2 + 4 * D].rearrange("p (kc m) -> p kc m", kc=4),
                    in_=src), waits=waits)
        return r

    ws_ready = stream_ws()
    w1_ready = stream_w1()
    w2_ready = stream_w2()

    def emit_ln_head(i, src_hbm, hist, first_waits, xa_dep,
                     xb_deps=(), xq_deps=()):
        ld = p.dma("sync", lambda e, i=i: e.dma_start(
            out=xa(i)[:], in_=fm_tile(src_hbm, i)),
            waits=list(first_waits) if i < 2 else [xa_dep])
        hist["ld"][i] = ld
        xb = ABF[:, (i % 2) * 2048:(i % 2) * 2048 + 2048]
        xq = ABF[:, 4096 + (i % 2) * 2048:4096 + (i % 2) * 2048 + 2048]
        cp = p.op("scalar", lambda e, xb=xb, i=i: e.activation(xb[:], xa(i)[:], AF.Copy),
                  waits=[ld, hist["mm_s"].get(i - 2)] + list(xb_deps))
        sq = p.op("scalar", lambda e, xq=xq, i=i: e.activation(xq[:], xa(i)[:], AF.Square),
                  waits=[ld, hist["mm_q"].get(i - 2)] + list(xq_deps))
        ps_s = pb[6 + i % 2][0:1, 0:T]
        ps_q = pb[6 + i % 2][0:1, T:2 * T]
        for c in range(NCH):
            mm = p.op("tensor", lambda e, c=c, xb=xb, ps=ps_s: e.matmul(
                ps, c_ones_c[:], xb[:, c * T:(c + 1) * T],
                start=(c == 0), stop=False, skip_group_check=True),
                waits=[cp, hist["e2"].get(i - 2)] if c == 0 else [],
                inc=(c == NCH - 1))
        hist["mm_s"][i] = mm
        for c in range(NCH):
            mm = p.op("tensor", lambda e, c=c, xq=xq, ps=ps_q: e.matmul(
                ps, c_ones_c[:], xq[:, c * T:(c + 1) * T],
                start=False, stop=(c == NCH - 1), skip_group_check=True),
                waits=[sq] if c == 0 else [],
                inc=(c == NCH - 1))
        hist["mm_q"][i] = mm

    def emit_ln_tail(i, hist, xln_dep, bm_extra, br_extra):
        ps_s = pb[6 + i % 2][0:1, 0:T]
        ps_q = pb[6 + i % 2][0:1, T:2 * T]
        nm = AFA[0:1, 7168 + (i % 2) * T:7168 + (i % 2) * T + T]
        vr = AFA[0:1, 7680 + (i % 2) * T:7680 + (i % 2) * T + T]
        hist["e1"][i] = p.op("scalar", lambda e, nm=nm, ps=ps_s: e.activation(
            nm, ps, AF.Copy, scale=-1.0 / D),
            waits=[hist["mm_q"][i], hist["br"].get(i - 2)])
        mb, rb = pb[4][:, 0:T], pb[5][:, 0:T]
        hist["bm"][i] = p.op("tensor", lambda e, nm=nm, mb=mb: e.matmul(
            mb, c_ones_r128[:], nm, start=True, stop=True, skip_group_check=True),
            waits=[hist["e1"][i], hist["tt1"].get(i - 1)] + list(bm_extra(i)),
            inc=True)
        hist["e2"][i] = p.op("scalar", lambda e, vr=vr, ps=ps_q: e.activation(
            vr, ps, AF.Copy, scale=1.0 / D),
            waits=[hist["br"].get(i - 2)])
        p.op("vector", lambda e, nm=nm: e.tensor_mul(nm, nm, nm),
             waits=[hist["bm"][i]])
        vv = p.op("vector", lambda e, vr=vr, nm=nm: e.tensor_sub(vr, vr, nm),
                  waits=[hist["e2"][i]])
        vv = p.op("vector", lambda e, vr=vr: e.tensor_scalar_add(vr, vr, 1e-5))
        sx = p.op("scalar", lambda e, vr=vr, nm=nm: e.activation(nm, vr, AF.Sqrt),
                  waits=[vv])
        hist["rr"][i] = sx
        hist["br"][i] = p.op("tensor", lambda e, nm=nm, rb=rb: e.matmul(
            rb, c_ones_r128[:], nm, start=True, stop=True, skip_group_check=True),
            waits=[sx, hist["tt2"].get(i - 1)] + list(br_extra(i)), inc=True)
        rc2 = p.op("vector", lambda e, rb=rb: e.reciprocal(rb, rb),
                   waits=[hist["br"][i]])
        hist["tt1"][i] = p.op("vector", lambda e, i=i, mb=mb: e.tensor_tensor(
            tevac3, xa3(i), bcast_mid(tevac3, mb), ALU.add),
            waits=[hist["ld"][i], hist["bm"][i]])
        hist["tt2"][i] = p.op("vector", lambda e, i=i, rb=rb: e.tensor_tensor(
            xlnb3(i), tevac3, bcast_mid(tevac3, rb), ALU.mult),
            waits=[xln_dep])
        return hist["tt2"][i]

    def ffn_pass(src_hbm, dst_hbm, first_waits, parts="ABC", wdeps=None):
        w1r, w2r, wsr = wdeps if wdeps else (w1_ready, w2_ready, ws_ready)
        hist = {k: {} for k in ("mm_s", "mm_q", "e1", "e2", "bm", "br",
                                "tt1", "tt2", "rr", "ld")}
        vm = next((w for w in first_waits if w and w[0] == "vector"), None)
        hist["tt2"][-1] = vm
        hist["tt1"][-1] = vm
        stx_h, sg_h, gd_h = {}, {}, {}
        ev_pb1 = None
        gd_prev = None
        gmm_h = {}

        def ffn_head(k):
            if k < NT:
                emit_ln_head(k, src_hbm, hist, first_waits,
                             hist["tt1"].get(k - 2),
                             xb_deps=[stx_h.get(k - 2), gmm_h.get(k - 2)])
        for k in range(NT + 2):
            if k < 2:
                ffn_head(k)
                continue
            i = k - 2
            tt2 = emit_ln_tail(i, hist, stx_h.get(i - 2),
                               lambda i: [], lambda i: [])
            stx_h[i] = p.dma("sync", lambda e, i=i: e.dma_start(
                out=fm_tile(xln_hbm, i), in_=xlnb(i)[:]), waits=[tt2])
            for c in range(NCH):
                mm = p.op("tensor", lambda e, c=c, i=i: e.matmul(
                    pb[0][0:8, 0:T], c_gw[:, c * E:(c + 1) * E],
                    xlnb(i)[:, c * T:(c + 1) * T],
                    start=(c == 0), stop=(c == NCH - 1), skip_group_check=True),
                    waits=[tt2, gw_ready, sg_h.get(i - 1)] if c == 0 else [],
                    inc=(c == NCH - 1))
            gmm_h[i] = mm
            ssb = AFA[0:8, 6144:6144 + T]
            sg_h[i] = p.op("scalar", lambda e, ssb=ssb: e.activation(
                ssb, pb[0][0:8, 0:T], AF.Sigmoid), waits=[mm])
            ts = p.op("vector", lambda e, ssb=ssb: e.tensor_scalar_mul(
                ssb, ssb, c_rs[0:8, 0:1]), waits=[sg_h[i], p0_done_v])
            for tc in range(T // 128):
                sl = ssb[:, tc * 128:(tc + 1) * 128]
                tr = p.op("tensor", lambda e, sl=sl, tc=tc: e.transpose(
                    pb[1][:, tc * 8:tc * 8 + 8], sl, c_id[0:8, 0:8]),
                    waits=[ts, ev_pb1], inc=True)
                sc_t = AFA[:, 6400 + tc * 8:6400 + tc * 8 + 8]
                ev_pb1 = p.op("vector", lambda e, sc_t=sc_t, tc=tc: e.tensor_copy(
                    sc_t, pb[1][:, tc * 8:tc * 8 + 8]), waits=[tr])
                top8 = AFA[:, 6416 + tc * 8:6416 + tc * 8 + 8]
                p.op("vector", lambda e, top8=top8, sc_t=sc_t: e.max(top8, sc_t))
                dsum = AFA[:, 6436 + tc * 2:6436 + tc * 2 + 1]
                dinv = AFA[:, 6437 + tc * 2:6437 + tc * 2 + 1]
                p.op("vector", lambda e, dsum=dsum, top8=top8: e.tensor_add(
                    dsum, top8[:, 0:1], top8[:, 1:2]))
                p.op("vector", lambda e, dinv=dinv, dsum=dsum: e.reciprocal(dinv, dsum))
                p.op("vector", lambda e, top8=top8: e.memset(top8[:, 2:8], -1e30))
                mrp = AFA[:, 6444 + tc * 8:6444 + tc * 8 + 8]
                p.op("vector", lambda e, mrp=mrp, top8=top8, sc_t=sc_t: e.match_replace(
                    mrp, top8, sc_t, 0.0))
                p.op("vector", lambda e, mrp=mrp, sc_t=sc_t: e.tensor_sub(mrp, sc_t, mrp))
                gt = p.op("vector", lambda e, mrp=mrp, dinv=dinv: e.tensor_scalar_mul(
                    mrp, mrp, dinv))
                tr2 = p.op("tensor", lambda e, mrp=mrp, tc=tc: e.transpose(
                    pb[2][0:8, tc * 128:tc * 128 + 128], mrp, c_id[:]),
                    waits=[gt, gd_prev], inc=True)
                gd_prev = p.op("vector", lambda e, tc=tc, i=i: e.tensor_copy(
                    gbuf[0:8, i * T + tc * 128:i * T + tc * 128 + 128],
                    pb[2][0:8, tc * 128:tc * 128 + 128]), waits=[tr2])
            gd_h[i] = gd_prev
            ffn_head(k)
        def hv(par, c):
            return pb[par * 2 + c // 2][:, (c % 2) * T:(c % 2) * T + T]

        def ov(c):
            return pb[4 + c // 2][:, (c % 2) * T:(c % 2) * T + T]

        st_c = {}
        xln_use = {}
        ev_prev = None
        gl_done = [None, None]
        gm_done = [None, None]
        ffn2_done = {}
        if parts == "ABC1a":
            pass
        NTC = 1 if parts in ("ABC1", "ABC1a", "ABC1b", "ABC1c") else NT if "C" in parts else int(parts[3:]) if parts.startswith("ABC") and len(parts) > 3 else NT
        for i in range(NTC):
            ldx = p.dma("sync", lambda e, i=i: e.dma_start(out=xa(i)[:], in_=fm_tile(src_hbm, i)),
                        waits=[p.mark("vector")] if i < 2 else [st_c[i - 2]])
            ldl = p.dma("sync", lambda e, i=i: e.dma_start(
                out=xlnb(i)[:], in_=fm_tile(xln_hbm, i)),
                waits=[stx_h[NT - 1]] if i < 2 else [xln_use[i - 2]])
            w0 = [gd_h[i], b2_ready]
            if i == 0:
                w0 += [p.mark("vector"), p.mark("scalar")]
            else:
                w0 += [gl_done[0], gl_done[1], gm_done[0], gm_done[1]]
            for ee in range(E):
                mm = p.op("tensor", lambda e, ee=ee, i=i: e.matmul(
                    hv(ee // 4, ee % 4), c_oh8[:, ee * 128:(ee + 1) * 128],
                    gbuf[0:8, i * T:(i + 1) * T], start=True, stop=True,
                    skip_group_check=True),
                    waits=w0 if ee == 0 else [], inc=(ee == E - 1))
            for ee in range(E):
                p.op("vector", lambda e, ee=ee: e.tensor_copy(
                    gb_sb[:, ee * T:(ee + 1) * T], hv(ee // 4, ee % 4)), waits=[mm])
            gb_done = p.mark("vector")
            for hc in range(4):
                for kc in range(NCH):
                    mm = p.op("tensor", lambda e, hc=hc, kc=kc, i=i: e.matmul(
                        hv(0, hc), Ws1v(kc, hc), xlnb(i)[:, kc * T:(kc + 1) * T],
                        start=(kc == 0 and hc % 2 == 0), stop=False,
                        skip_group_check=True),
                        waits=[gb_done, ldl, wsr] if (hc == 0 and kc == 0) else [],
                        inc=(hc == 3 and kc == NCH - 1))
            s1_mm = mm
            for hc in range(4):
                sg1 = p.op("scalar", lambda e, hc=hc: e.activation(
                    s1_sb[:, hc * T:(hc + 1) * T], hv(0, hc), AF.Gelu,
                    bias=c_bs1[:, hc:hc + 1]),
                    waits=[s1_mm] if hc == 0 else [])
            if parts == "ABC1a":
                dbg = [p.mark("vector"), p.mark("scalar")]
                p.dma("gpsimd", lambda e: e.dma_start(
                    out=dst_hbm.ap()[1], in_=gb_sb[:]), waits=dbg)
                p.dma("gpsimd", lambda e: e.dma_start(
                    out=dst_hbm.ap()[2][:, 0:1024], in_=s1_sb[:]), waits=dbg)
                p.dma("gpsimd", lambda e: e.dma_start(
                    out=dst_hbm.ap()[5][:, 0:2048], in_=xlnb(0)[:]), waits=dbg)
                p._wait("sync", p.mark("dgp"))
                return None, None, None, None
            for c in range(NCH):
                for kc in range(4):
                    mm = p.op("tensor", lambda e, c=c, kc=kc: e.matmul(
                        ov(c), Ws2v(kc, c), s1_sb[:, kc * T:(kc + 1) * T],
                        start=(kc == 0 and c % 2 == 0), stop=False,
                        skip_group_check=True),
                        waits=[sg1, ev_prev] if (c == 0 and kc == 0) else [],
                        inc=(c == NCH - 1 and kc == 3))
            s2_mm = mm
            for c in range(NCH):
                p.op("tensor", lambda e, c=c, i=i: e.matmul(
                    ov(c), c_b2bf[:, c * 128:(c + 1) * 128],
                    gbuf[0:8, i * T:(i + 1) * T], start=False, stop=False,
                    skip_group_check=True))

            if parts == "ABC1b":
                for c in range(NCH):
                    ev = p.op("scalar", lambda e, c=c: e.activation(
                        tevac[:, c * T:(c + 1) * T], ov(c), AF.Identity,
                        bias=c_bs2h[:, c:c + 1], scale=0.5),
                        waits=[s2_mm, p.mark("tensor")] if c == 0 else [])
                p.dma("sync", lambda e: e.dma_start(
                    out=dst_hbm.ap()[4], in_=tevac[:]), waits=[ev])
                p._wait("sync", p.mark("dsync"))
                return None, None, None, None

            def emit_ffn1(ee):
                par = ee % 2
                w = []
                if ee == 0:
                    w = [sg1, gb_done]
                elif ee == 1:
                    w = [gb_done]
                elif gl_done[par] is not None:
                    w = [gl_done[par]]
                mm = None
                for hc in range(4):
                    for kc in range(NCH):
                        mm = p.op("tensor", lambda e, ee=ee, hc=hc, kc=kc, i=i: e.matmul(
                            hv(par, hc), W1v(ee, kc, hc),
                            xlnb(i)[:, kc * T:(kc + 1) * T],
                            start=(kc == 0 and hc % 2 == 0), stop=False,
                            skip_group_check=True),
                            waits=(w + [w1r[ee]]) if (hc == 0 and kc == 0) else [],
                            inc=(hc == 3 and kc == NCH - 1))
                return mm

            def emit_tail(ee):
                par = ee % 2
                for hc in range(4):
                    gl = p.op("scalar", lambda e, ee=ee, hc=hc, par=par: e.activation(
                        h_g[:, hc * T:(hc + 1) * T], hv(par, hc), AF.Gelu,
                        bias=c_b1[:, ee * 4 + hc:ee * 4 + hc + 1]),
                        waits=([h1_mm[ee]] + ([gm_done[1 - par]] if gm_done[1 - par] else []))
                        if hc == 0 else [])
                gl_done[par] = gl
                h_g3 = h_g.rearrange("p (c t) -> p c t", c=4)
                h_s3 = h_s.rearrange("p (c t) -> p c t", c=4)
                gm = p.op("vector", lambda e, ee=ee: e.tensor_tensor(
                    h_s3, h_g3, bcast_mid(h_g3, gb_sb[:, ee * T:(ee + 1) * T]), ALU.mult),
                    waits=[gl, ffn2_done.get(ee - 1)])
                gm_done[par] = gm
                mm = None
                for c in range(NCH):
                    for kc in range(4):
                        mm = p.op("tensor", lambda e, ee=ee, c=c, kc=kc: e.matmul(
                            ov(c), W2v(ee, kc, c), h_s[:, kc * T:(kc + 1) * T],
                            start=False, stop=(ee == E - 1 and c == NCH - 1 and kc == 3),
                            skip_group_check=True),
                            waits=([gm, w2r[ee]] if (c == 0 and kc == 0) else []),
                            inc=(c == NCH - 1 and kc == 3))
                ffn2_done[ee] = mm

            h1_mm = {}
            h1_mm[0] = emit_ffn1(0)
            if parts == "ABC1c":
                emit_tail(0)
                acc1 = ffn2_done[0]
                for c in range(NCH):
                    ev = p.op("scalar", lambda e, c=c: e.activation(
                        tevac[:, c * T:(c + 1) * T], ov(c), AF.Identity,
                        bias=c_bs2h[:, c:c + 1], scale=0.5),
                        waits=[acc1] if c == 0 else [])
                dbg = [ev, p.mark("vector")]
                p.dma("sync", lambda e: e.dma_start(
                    out=dst_hbm.ap()[4], in_=tevac[:]), waits=dbg)
                p.dma("gpsimd", lambda e: e.dma_start(
                    out=dst_hbm.ap()[3][:, 0:1024], in_=h_g[:]), waits=dbg)
                p.dma("gpsimd", lambda e: e.dma_start(
                    out=dst_hbm.ap()[3][:, 1024:2048], in_=h_s[:]), waits=dbg)
                p.dma("sync", lambda e: e.dma_start(
                    out=dst_hbm.ap()[6], in_=AFA[:, 0:2048]), waits=dbg)
                for a in range(4):
                    p.dma("gpsimd", lambda e, a=a: e.dma_start(
                        out=dst_hbm.ap()[8 + a],
                        in_=wbuf1[:, a * 2048:(a + 1) * 2048]), waits=dbg)
                p._wait("sync", p.mark("dsync"))
                p._wait("sync", p.mark("dgp"))
                return None, None, None, None
            for ee in range(1, E):
                h1_mm[ee] = emit_ffn1(ee)
                emit_tail(ee - 1)
            emit_tail(E - 1)
            acc = ffn2_done[E - 1]
            for c in range(NCH):
                ev = p.op("scalar", lambda e, c=c: e.activation(
                    tevac[:, c * T:(c + 1) * T], ov(c), AF.Identity,
                    bias=c_bs2h[:, c:c + 1], scale=0.5),
                    waits=[acc] if c == 0 else [])
            ev_prev = ev
            ad = p.op("vector", lambda e, i=i: e.tensor_add(xa(i)[:], xa(i)[:], tevac[:]),
                      waits=[ldx, ev])
            st_c[i] = p.dma("sync", lambda e, i=i: e.dma_start(
                out=fm_tile(dst_hbm, i), in_=xa(i)[:]), waits=[ad])
            xln_use[i] = h1_mm[E - 1]
            if parts == "ABC1" and i == 0:
                dbg = [p.mark("vector"), p.mark("scalar"), st_c[0]]
                p.dma("gpsimd", lambda e: e.dma_start(
                    out=dst_hbm.ap()[1], in_=gb_sb[:]), waits=dbg)
                p.dma("gpsimd", lambda e: e.dma_start(
                    out=dst_hbm.ap()[2][:, 0:1024], in_=s1_sb[:]), waits=dbg)
                p.dma("gpsimd", lambda e: e.dma_start(
                    out=dst_hbm.ap()[3][:, 0:1024], in_=h_g[:]), waits=dbg)
                p.dma("gpsimd", lambda e: e.dma_start(
                    out=dst_hbm.ap()[3][:, 1024:2048], in_=h_s[:]), waits=dbg)
                p.dma("sync", lambda e: e.dma_start(
                    out=dst_hbm.ap()[4], in_=tevac[:]), waits=dbg)
                p.dma("gpsimd", lambda e: e.dma_start(
                    out=dst_hbm.ap()[5][:, 0:2048], in_=xlnb(0)[:]), waits=dbg)
        return st_c[NTC - 1], p.mark("vector"), p.mark("scalar"), p.mark("tensor")


    def attn_pass(src_hbm, dst_hbm, first_waits, attn_ready, aparts="ABCD"):
        bv_b = AFA[:, 6144:7168]
        ld_bvr = p.dma("sync", lambda e: e.dma_start(
            out=AFA[0:1, 4352:4352 + D], in_=dram["bv_row"][:]), waits=list(first_waits))
        mbv0 = p.op("tensor", lambda e: e.matmul(
            pb[0][:, 0:512], c_ones_r128[:], AFA[0:1, 4352:4864], start=True,
            stop=True, skip_group_check=True),
            waits=[ld_bvr] + list(first_waits), inc=True)
        mbv1 = p.op("tensor", lambda e: e.matmul(
            pb[1][:, 0:512], c_ones_r128[:], AFA[0:1, 4864:5376], start=True,
            stop=True, skip_group_check=True), inc=True)
        p.op("vector", lambda e: e.tensor_copy(bv_b[:, 0:512], pb[0][:, 0:512]),
             waits=[mbv0])
        bv_done = p.op("vector", lambda e: e.tensor_copy(bv_b[:, 512:D],
                                                         pb[1][:, 0:512]),
                       waits=[mbv1])

        hist = {k: {} for k in ("mm_s", "mm_q", "e1", "e2", "bm", "br",
                                "tt1", "tt2", "rr", "ld")}
        hist["tt2"][-1] = bv_done
        hist["tt1"][-1] = bv_done
        qsl = [ABF[:, 4096 + k * 256:4096 + (k + 1) * 256] for k in range(4)]
        ksl = [ABF[:, 5120 + k * 256:5120 + (k + 1) * 256] for k in range(4)]
        vsl = [ABF[:, 6144 + k * 1024:6144 + (k + 1) * 1024] for k in range(2)]
        qst_h, kst_h, vst_h, qev_h, kev_h, vev_h = {}, {}, {}, {}, {}, {}
        vmm_h = {}
        last_qkv_mm = {}
        def attn_head(k):
            if k < NT:
                xqd = ([qst_h.get((k - 2) * NCH + 7), kst_h.get((k - 2) * NCH + 7)]
                       if k % 2 == 0 else [vst_h.get(2 * (k - 2) + 1)])
                emit_ln_head(k, src_hbm, hist, list(first_waits) + [bv_done],
                             hist["tt1"].get(k - 2),
                             xb_deps=[last_qkv_mm.get(k - 2)],
                             xq_deps=xqd)
        for k in range(NT + 2):
            if k < 2:
                attn_head(k)
                continue
            i = k - 2
            tt2 = emit_ln_tail(
                i, hist, last_qkv_mm.get(i - 2),
                lambda i: [vev_h.get((2 * i - 1, 0))],
                lambda i: [vev_h.get((2 * i - 1, 1))])
            for W, bias, scale, dsthbm, sb_slots, ev_h, st_hh, c0buf, pbase in (
                ("q", c_bq8, 0.125, qs_hbm, qsl, qev_h, qst_h, c_q0, 0),
                ("k", c_bk, 1.0, ks_hbm, ksl, kev_h, kst_h, c_k0, 2),
            ):
                for c in range(NCH):
                    key = i * NCH + c
                    slot = key % 4
                    ps = pb[pbase + slot % 2][:, (slot // 2) * T:(slot // 2) * T + T]
                    for kc in range(NCH):
                        mm = p.op("tensor", lambda e, c=c, kc=kc, i=i, ps=ps, W=W:
                                  e.matmul(
                            ps, (Wqv if W == "q" else Wkv)(kc, c),
                            xlnb(i)[:, kc * T:(kc + 1) * T],
                            start=(kc == 0), stop=(kc == NCH - 1),
                            skip_group_check=True),
                            waits=([tt2, attn_ready, ev_h.get(key - 2)])
                            if kc == 0 else [],
                            inc=(kc == NCH - 1))
                    sbuf = sb_slots[slot]
                    ev_h[key] = p.op("scalar", lambda e, sbuf=sbuf, ps=ps, bias=bias,
                                     scale=scale, c=c: e.activation(
                        sbuf, ps, AF.Identity, bias=bias[:, c:c + 1], scale=scale),
                        waits=[mm, st_hh.get(key - 4)])
                    st_hh[key] = p.dma("sync", lambda e, sbuf=sbuf, c=c, i=i,
                                       dsthbm=dsthbm: e.dma_start(
                        out=dsthbm.ap()[c * 128:(c + 1) * 128, i * T:(i + 1) * T],
                        in_=sbuf), waits=[ev_h[key]])
                    if i == 0:
                        p.op("vector", lambda e, sbuf=sbuf, c=c, c0buf=c0buf:
                             e.tensor_copy(c0buf[:, c:c + 1], sbuf[:, 0:1]),
                             waits=[ev_h[key]])
            for tc in range(T // 128):
                tk = i * 2 + tc
                vbuf = vsl[tk % 2]
                for half in range(2):
                    ps = pb[4 + half]
                    for kc in range(NCH):
                        mm = p.op("tensor", lambda e, tc=tc, half=half, kc=kc, i=i,
                                  ps=ps: e.matmul(
                            ps[:, 0:512],
                            xlnb(i)[:, kc * T + tc * 128:kc * T + tc * 128 + 128],
                            Wvv(kc, half),
                            start=(kc == 0), stop=(kc == NCH - 1),
                            skip_group_check=True),
                            waits=([tt2] + [vev_h.get((tk - 1, half))])
                            if kc == 0 else [],
                            inc=(kc == NCH - 1))
                    vmm_h[(tk, half)] = mm
                    vev_h[(tk, half)] = p.op("vector", lambda e, vbuf=vbuf, half=half,
                                             ps=ps: e.tensor_tensor(
                        vbuf[:, half * 512:half * 512 + 512], ps[:, 0:512],
                        bv_b[:, half * 512:half * 512 + 512], ALU.add),
                        waits=[mm, vst_h.get(tk - 2)])
                vst_h[tk] = p.dma("sync", lambda e, i=i, tc=tc, vbuf=vbuf: e.dma_start(
                    out=vs_hbm.ap()[i * T + tc * 128:i * T + tc * 128 + 128, :],
                    in_=vbuf), waits=[vev_h[(tk, 1)]])
            last_qkv_mm[i] = vmm_h[(i * 2 + 1, 1)]
            attn_head(k)
        qkv_done = [p.mark("dsync"), p.mark("vector"), p.mark("scalar"),
                    p.mark("tensor")]
        q0bd = ABF[:, 8704:8832]
        ms = p.op("vector", lambda e: e.memset(q0bd, 0.0), waits=qkv_done[1:2])
        for c in range(NCH):
            p.op("vector", lambda e, c=c: e.tensor_copy(
                q0bd[0:64, c * 16 + 2 * c:c * 16 + 2 * c + 1], c_q0[0:64, c:c + 1]))
            p.op("vector", lambda e, c=c: e.tensor_copy(
                q0bd[64:128, c * 16 + 2 * c + 1:c * 16 + 2 * c + 2],
                c_q0[64:128, c:c + 1]))
        q0bd_done = p.mark("vector")
        gw_sb = AFA[0:16, 0:S]
        kblk = [wbuf1[:, 0:4096], wbuf1[:, 4096:8192]]
        gev_h, gld_h = {}, {}
        for sb in range(NCH):
            ldk = p.dma("sync", lambda e, sb=sb: e.dma_start(
                out=kblk[sb % 2][:].rearrange("p (c s) -> p c s", c=NCH),
                in_=ks_hbm.ap().rearrange("(c p) s -> p c s", p=128)
                    [:, :, sb * 512:(sb + 1) * 512]),
                waits=qkv_done[0:1] + ([gev_h.get(sb - 2)] if sb >= 2 else
                                       [p.mark("tensor")]))
            gld_h[sb] = ldk
            for c in range(NCH):
                mm = p.op("tensor", lambda e, sb=sb, c=c: e.matmul(
                    pb[0][0:16, 0:512], q0bd[:, c * 16:(c + 1) * 16],
                    kblk[sb % 2][:].rearrange("p (c s) -> p c s", c=NCH)[:, c, :],
                    start=(c == 0), stop=(c == NCH - 1), skip_group_check=True),
                    waits=([ldk, q0bd_done] + ([gev_h.get(sb - 1)] if sb >= 1 else []))
                    if c == 0 else [],
                    inc=(c == NCH - 1))
            gev_h[sb] = p.op("scalar", lambda e, sb=sb: e.activation(
                gw_sb[:, sb * 512:(sb + 1) * 512], pb[0][0:16, 0:512], AF.Copy),
                waits=[mm])
        negmax = AFA[0:16, 4096:4097]
        gsum = AFA[0:16, 4097:4098]
        ginv = AFA[0:16, 4098:4099]
        p.op("vector", lambda e: e.tensor_reduce(negmax, gw_sb, AX.X, ALU.max,
                                                 negate=True),
             waits=[gev_h[NCH - 1]])
        ex = p.op("scalar", lambda e: e.activation(gw_sb, gw_sb, AF.Exp, bias=negmax,
                                                   accum_out=gsum),
                  waits=[p.mark("vector")])
        p.op("vector", lambda e: e.reciprocal(ginv, gsum), waits=[ex])
        gpT = ABF[:, 0:512]
        tr_prev = None
        for j in range(32):
            tr = p.op("tensor", lambda e, j=j: e.transpose(
                pb[1][:, 0:16], gw_sb[:, j * 128:(j + 1) * 128], c_id[0:16, 0:16]),
                waits=[ex, tr_prev] if j == 0 else [tr_prev], inc=True)
            tr_prev = p.op("vector", lambda e, j=j: e.tensor_copy(
                gpT[:, j * 16:(j + 1) * 16], pb[1][:, 0:16]), waits=[tr])
        vblk = [wbuf1[:, 8192:9216], wbuf1[:, 9216:10240]]
        pvld_h, pvmm_h = {}, {}
        for j in range(32):
            ldv = p.dma("sync", lambda e, j=j: e.dma_start(
                out=vblk[j % 2][:], in_=vs_hbm.ap()[j * 128:(j + 1) * 128, :]),
                waits=[pvmm_h[j - 2]] if j >= 2 else [p.mark("tensor")])
            for half in range(2):
                mm = p.op("tensor", lambda e, j=j, half=half: e.matmul(
                    pb[2 + half][0:16, 0:512], gpT[:, j * 16:(j + 1) * 16],
                    vblk[j % 2][:, half * 512:half * 512 + 512],
                    start=(j == 0), stop=(j == 31), skip_group_check=True),
                    waits=[ldv, tr_prev] if half == 0 else [],
                    inc=(half == 1))
            pvmm_h[j] = mm
        og = AFA[0:16, 4608:5632]
        m1 = p.op("vector", lambda e: e.tensor_scalar_mul(
            og[:, 0:512], pb[2][0:16, 0:512], ginv), waits=[pvmm_h[31]])
        m2 = p.op("vector", lambda e: e.tensor_scalar_mul(
            og[:, 512:1024], pb[3][0:16, 0:512], ginv))
        msk16 = AFA[0:16, 5632:6656]
        ldm = p.dma("sync", lambda e: e.dma_start(out=msk16, in_=dram["mask16"][:]),
                    waits=qkv_done[1:2])
        mm1 = p.op("vector", lambda e: e.tensor_mul(og, og, msk16), waits=[ldm, m2])
        gctx = AFA[:, 6656:6664]
        tr_prev2 = mm1
        for c in range(NCH):
            tr = p.op("tensor", lambda e, c=c: e.transpose(
                pb[4][:, 0:16], og[:, c * 128:(c + 1) * 128], c_id[0:16, 0:16]),
                waits=[tr_prev2], inc=True)
            cp = p.op("vector", lambda e, c=c: e.tensor_copy(
                AFA[:, 6664:6680], pb[4][:, 0:16]), waits=[tr])
            tr_prev2 = p.op("vector", lambda e, c=c: e.tensor_reduce(
                gctx[:, c:c + 1], AFA[:, 6664:6680], AX.X, ALU.add))
        gctx_b = ABF[:, 8832:8840]
        gcb = p.op("vector", lambda e: e.tensor_copy(gctx_b, gctx))
        ldx0 = p.dma("sync", lambda e: e.dma_start(out=xa(0)[:], in_=fm_tile(src_hbm, 0)),
                     waits=[p.mark("vector")])
        for c in range(NCH):
            for kc in range(NCH):
                mm = p.op("tensor", lambda e, c=c, kc=kc: e.matmul(
                    pb[5][:, c:c + 1], Wov(kc, c), gctx_b[:, kc:kc + 1],
                    start=(kc == 0 and c == 0), stop=(c == NCH - 1 and kc == NCH - 1),
                    skip_group_check=True),
                    waits=[gcb] if (c == 0 and kc == 0) else [],
                    inc=(c == NCH - 1 and kc == NCH - 1))
        for c in range(NCH):
            t0e = p.op("scalar", lambda e, c=c: e.activation(
                AFA[:, 6680 + c:6681 + c], pb[5][:, c:c + 1], AF.Identity,
                bias=c_bo[:, c:c + 1]), waits=[mm] if c == 0 else [])
        t0a = p.op("vector", lambda e: e.tensor_tensor(
            AFA[:, 6688:6696].rearrange("p (c a) -> p c a", a=1),
            xa(0)[:].rearrange("p (c t) -> p c t", c=NCH)[:, :, 0:1],
            AFA[:, 6680:6688].rearrange("p (c a) -> p c a", a=1),
            ALU.add), waits=[ldx0, t0e])
        for c in range(NCH):
            p.dma("sync", lambda e, c=c: e.dma_start(
                out=dst_hbm.ap()[0][:, c * T:c * T + 1],
                in_=AFA[:, 6688 + c:6689 + c],
                allow_slow_non_contiguous=True), waits=[t0a])
        glob_done = [p.mark("dsync"), p.mark("vector"), p.mark("scalar"),
                     p.mark("tensor")]
        if aparts == "ABC":
            return glob_done

        qbuf = [wbuf1[:, 0:1024], wbuf1[:, 1024:2048]]
        kbuf = [wbuf1[:, 2048:5120], wbuf1[:, 5120:8192]]
        vbuf = [wbuf1[:, 8192:11264], wbuf1[:, 11264:14336]]
        ET = [wbuf1[:, 14336:17408], wbuf1[:, 17408:20480]]
        ctx_tok = wbuf1[:, 20480:21504]
        ctxg2 = [wbuf1[:, 21504:25600], wbuf1[:, 25600:29696]]
        rinvT = AFA[:, 6144:6176]
        tevc = AFA[:, 6176:6688]
        x1c = [AFA[:, 6688:7200], AFA[:, 7200:7712]]

        stq_h, pvw_h, trp_h = {}, {}, {}
        woev_h, womm_h, wost_h, ad_h = {}, {}, {}, {}
        rc_h, ld_h = {}, {}
        ETS = [ET[0][:, k * 768:(k + 1) * 768] for k in range(4)] + \
              [ET[1][:, k * 768:(k + 1) * 768] for k in range(4)]
        rslot_guard = [glob_done[2]] * 2
        ets_guard = [glob_done[3]] * 8
        cmb_prev = [glob_done[1]]
        cmb_prev_hi = [glob_done[1]]
        trp_guard = None
        unitctr = [0]
        NJ = 31
        DLVL = 9
        if aparts.startswith("ABCD") and len(aparts) > 4:
            arg = aparts[4:]
            if arg.startswith("L"):
                DLVL = int(arg[1:])
                NJ = 1
            else:
                NJ = int(arg)

        def jparams(j):
            ks0 = 1 if j == 0 else 128 * (j - 1) + 1
            nkc = 2 if j in (0, 30) else 3
            return ks0, nkc, nkc * 128

        def emit_loads(jx):
            ks0, nkc, kw = jparams(jx)
            ldq = p.dma("sync", lambda e, jx=jx: e.dma_start(
                out=qbuf[jx % 2][:].rearrange("p (c u) -> p c u", c=NCH),
                in_=qs_hbm.ap().rearrange("(c p) s -> p c s", p=128)
                    [:, :, 128 * jx + 1:128 * jx + 129]),
                waits=(glob_done[0:1] if jx < 2 else []) +
                      ([stq_h[jx - 2]] if jx >= 2 else glob_done[3:4]))
            ldk = p.dma("sync", lambda e, jx=jx, ks0=ks0, kw=kw: e.dma_start(
                out=kbuf[jx % 2][:, 0:NCH * kw].rearrange("p (c u) -> p c u", c=NCH),
                in_=ks_hbm.ap().rearrange("(c p) s -> p c s", p=128)
                    [:, :, ks0:ks0 + kw]),
                waits=[stq_h[jx - 2]] if jx >= 2 else glob_done[3:4])
            ldv = p.dma("sync", lambda e, jx=jx, ks0=ks0, kw=kw: e.dma_start(
                out=vbuf[jx % 2][:, 0:NCH * kw].rearrange("p (b d) -> p b d", d=D),
                in_=vs_hbm.ap()[ks0:ks0 + kw, :].rearrange("(b p) d -> p b d", p=128)),
                waits=[pvw_h[jx - 2]] if jx >= 2 else glob_done[3:4])
            ld_h[jx] = (ldq, ldk, ldv)

        def emit_wo(g, jlast):
            ctxg = ctxg2[g % 2]
            if jlast == 30:
                mz = p.op("vector", lambda e, ctxg=ctxg: e.memset(
                    ctxg[:].rearrange("p (c u) -> p c u", c=NCH)[:, :, 384:512],
                    0.0), waits=[trp_h[jlast]])
                zdep = [mz]
            else:
                zdep = []
            wo_w = 511 if g == 7 else 512
            for c in range(NCH):
                wpar = unitctr[0] % 2
                unitctr[0] += 1
                woff = wpar * 1024
                for kc in range(NCH):
                    mm = p.op("tensor", lambda e, c=c, kc=kc, woff=woff,
                              ctxg=ctxg: e.matmul(
                        PB[:, woff:woff + 512], Wov(kc, c),
                        ctxg[:, kc * 512:(kc + 1) * 512],
                        start=(kc == 0), stop=(kc == NCH - 1),
                        skip_group_check=True),
                        waits=([trp_h[jlast], rslot_guard[wpar]] + zdep)
                        if kc == 0 else [],
                        inc=(kc == NCH - 1))
                womm_h[g] = mm
                ev = p.op("scalar", lambda e, c=c, woff=woff: e.activation(
                    tevc, PB[:, woff:woff + 512], AF.Identity,
                    bias=c_bo[:, c:c + 1]),
                    waits=[mm, p.mark("vector")] if c == 0 else
                          [mm, ad_h[(g, c - 1)]])
                woev_h[(g, c)] = ev
                rslot_guard[wpar] = ev
                pw = []
                pw.append(p.dma("sync", lambda e, g=g, c=c: e.dma_start(
                    out=x1c[c % 2][:, 0:255],
                    in_=src_hbm.ap()[2 * g][:, c * T + 1:(c + 1) * T]),
                    waits=[wost_h.get((g, c - 2))]))
                pw.append(p.dma("sync", lambda e, g=g, c=c: e.dma_start(
                    out=x1c[c % 2][:, 255:511],
                    in_=src_hbm.ap()[2 * g + 1][:, c * T:(c + 1) * T])))
                if g < 7:
                    pw.append(p.dma("sync", lambda e, g=g, c=c: e.dma_start(
                        out=x1c[c % 2][:, 511:512],
                        in_=src_hbm.ap()[2 * g + 2][:, c * T:c * T + 1],
                        allow_slow_non_contiguous=True)))
                ad = p.op("vector", lambda e, c=c, wo_w=wo_w: e.tensor_add(
                    x1c[c % 2][:, 0:wo_w], x1c[c % 2][:, 0:wo_w],
                    tevc[:, 0:wo_w]), waits=pw + [ev])
                ad_h[(g, c)] = ad
                s1_ = p.dma("sync", lambda e, g=g, c=c: e.dma_start(
                    out=dst_hbm.ap()[2 * g][:, c * T + 1:(c + 1) * T],
                    in_=x1c[c % 2][:, 0:255]), waits=[ad])
                s2_ = p.dma("sync", lambda e, g=g, c=c: e.dma_start(
                    out=dst_hbm.ap()[2 * g + 1][:, c * T:(c + 1) * T],
                    in_=x1c[c % 2][:, 255:511]))
                if g < 7:
                    s3_ = p.dma("sync", lambda e, g=g, c=c: e.dma_start(
                        out=dst_hbm.ap()[2 * g + 2][:, c * T:c * T + 1],
                        in_=x1c[c % 2][:, 511:512],
                        allow_slow_non_contiguous=True))
                    wost_h[(g, c)] = s3_
                else:
                    wost_h[(g, c)] = s2_

        emit_loads(0)
        if NJ > 1:
            emit_loads(1)
        pending_wo = None
        for j in range(NJ):
            g, jj = j // 4, j % 4
            ks0, nkc, kw = jparams(j)
            has_A = j >= 1
            has_B = j <= 29
            kcA = (0, 1)
            kcB = (0, 1) if j == 0 else (1, 2)
            cscale = 1.0 if j in (0, 30) else 0.5
            ldq, ldk, ldv = ld_h[j]
            kb3 = kbuf[j % 2][:, 0:NCH * kw].rearrange("p (c u) -> p c u", c=NCH)
            qb3 = qbuf[j % 2][:].rearrange("p (c u) -> p c u", c=NCH)
            vb3 = vbuf[j % 2][:, 0:NCH * kw].rearrange("p (b d) -> p b d", d=D)
            pv_half_lo = [None]
            if DLVL < 1:
                stq_h[j] = p.mark("tensor")
                continue
            ust = [None] * 8
            for u2 in range(9):
                if u2 < 8:
                    upar = unitctr[0] % 2
                    unitctr[0] += 1
                    uc = j * 8 + u2
                    roff = upar * 1024
                    ets = ETS[uc % 8]
                    for l in range(2):
                        h = 2 * u2 + l
                        po = 64 * (h % 2)
                        for kc in range(nkc):
                            w = []
                            if l == 0 and kc == 0:
                                w = [ldq, ldk, rslot_guard[upar]]
                                if j == 0 and u2 == 0:
                                    w += [glob_done[1], glob_done[2]]
                            mm = p.op("tensor", lambda e, h=h, l=l, kc=kc, po=po,
                                      roff=roff, kb3=kb3, qb3=qb3: e.matmul(
                                PB[:, roff + l * 512 + kc * 128:
                                   roff + l * 512 + kc * 128 + 128],
                                kb3[po:po + 64, h // 2, kc * 128:(kc + 1) * 128],
                                qb3[po:po + 64, h // 2, :],
                                start=(kc == 0), stop=(kc == nkc - 1),
                                skip_group_check=True),
                                waits=w, inc=(l == 1 and kc == nkc - 1))
                    st_mm = mm
                    if DLVL == 1:
                        rslot_guard[upar] = st_mm
                        continue
                    pview = PB[:, roff:roff + 1024].rearrange(
                        "p (l x) -> p l x", l=2)[:, :, 0:kw]
                    etv = ets[:, 0:2 * kw].rearrange("p (l x) -> p l x", l=2)
                    ex = p.op("scalar", lambda e, pview=pview, etv=etv:
                              e.activation(etv, pview, AF.Exp),
                              waits=[st_mm, ets_guard[uc % 8]])
                    if DLVL < 3:
                        rslot_guard[upar] = ex
                        continue
                    ust[u2] = (upar, roff, ets, ex, uc)
                if u2 >= 1 and DLVL >= 3 and ust[u2 - 1] is not None:
                    v2 = u2 - 1
                    vpar, roff, ets, ex, uc = ust[v2]
                    first = True
                    for l in range(2):
                        for si, (hs, kcp) in enumerate(((has_A, kcA),
                                                        (has_B, kcB))):
                            if not hs:
                                continue
                            for z, kc in enumerate(kcp):
                                w = [ex] if first else []
                                first = False
                                mm = p.op("tensor", lambda e, l=l, kc=kc, si=si,
                                          z=z, ets=ets, kw=kw, roff=roff:
                                          e.matmul(
                                    PB[:, roff + 896 + l * 2 + si:
                                       roff + 897 + l * 2 + si],
                                    ets[:, l * kw + kc * 128:
                                        l * kw + (kc + 1) * 128],
                                    c_ones_c[:],
                                    start=(z == 0), stop=(z == 1),
                                    skip_group_check=True),
                                    waits=w, inc=(l == 1 and z == 1))
                    sums_mm = mm
                    rc = p.op("vector", lambda e, v2=v2, roff=roff: e.reciprocal(
                        rinvT[:, v2 * 4:(v2 + 1) * 4],
                        PB[:, roff + 896:roff + 900]),
                        waits=[sums_mm])
                    if cscale != 1.0:
                        p.op("vector", lambda e, v2=v2, cscale=cscale:
                             e.tensor_scalar_mul(rinvT[:, v2 * 4:(v2 + 1) * 4],
                                                 rinvT[:, v2 * 4:(v2 + 1) * 4],
                                                 cscale))
                    rc_h[uc] = rc
                    rslot_guard[vpar] = rc
                    if DLVL < 4:
                        continue
                    for si, (hs, kcp) in enumerate(((has_A, kcA), (has_B, kcB))):
                        if not hs:
                            continue
                        coff = 2048 if si == 0 else 3072
                        for l in range(2):
                            h = 2 * v2 + l
                            for z, kc in enumerate(kcp):
                                w = []
                                if l == 0 and z == 0 and \
                                        si == (0 if has_A else 1):
                                    if v2 == 0:
                                        w = [ldv, cmb_prev[0]]
                                    elif v2 == 4:
                                        w = [cmb_prev_hi[0]]
                                mm = p.op("tensor", lambda e, l=l, h=h, kc=kc,
                                          coff=coff, v2=v2, ets=ets, vb3=vb3,
                                          kw=kw, z=z: e.matmul(
                                    PB[:, coff + v2 * 128 + l * 64:
                                       coff + v2 * 128 + (l + 1) * 64],
                                    ets[:, l * kw + kc * 128:
                                        l * kw + (kc + 1) * 128],
                                    vb3[:, kc, h * 64:(h + 1) * 64],
                                    start=(z == 0 and l == 0 and v2 % 4 == 0),
                                    stop=(z == 1), skip_group_check=True),
                                    waits=w, inc=(l == 1 and z == 1))
                    pvw_h[j] = mm
                    if v2 == 3:
                        pv_half_lo[0] = mm
                    ets_guard[uc % 8] = mm
            if DLVL < 4:
                stq_h[j] = st_mm
                continue
            stq_h[j] = st_mm
            sidxs = [s for s, hs in ((0, has_A), (1, has_B)) if hs]

            def rv_ap(si, half):
                return bass.AP(rinvT.tensor, rinvT.offset + half * 16 + si,
                               [rinvT.ap[0], [2, 8], [0, 64]])

            def pv_ap(si, half):
                coff = (2048 if si == 0 else 3072) + half * 512
                return PB[:, coff:coff + 512].rearrange("p (h d) -> p h d", h=8)

            two = has_A and has_B
            for half in range(2):
                hw = ([pv_half_lo[0], trp_guard] if half == 0
                      else [pvw_h[j]])
                for si in sidxs:
                    if two:
                        t1v = AFA[:, 4096 + si * 1024 + half * 512:
                                  4608 + si * 1024 + half * 512].rearrange(
                            "p (h d) -> p h d", h=8)
                    else:
                        t1v = ctx_tok[:, half * 512:(half + 1) * 512].rearrange(
                            "p (h d) -> p h d", h=8)
                    cmb = p.op("vector", lambda e, si=si, half=half, t1v=t1v:
                               e.tensor_tensor(t1v, pv_ap(si, half),
                                               rv_ap(si, half), ALU.mult),
                               waits=hw)
                    hw = []
                if two:
                    cmb = p.op("vector", lambda e, half=half: e.tensor_add(
                        ctx_tok[:, half * 512:(half + 1) * 512],
                        AFA[:, 4096 + half * 512:4608 + half * 512],
                        AFA[:, 5120 + half * 512:5632 + half * 512]))
                if half == 0:
                    cmb_prev[0] = cmb
            cmb_prev_hi[0] = cmb
            if j + 2 < NJ:
                emit_loads(j + 2)
            for c in range(NCH):
                trp = p.dma("sync", lambda e, c=c, jj=jj, g=g: e.dma_start(
                    out=ctxg2[g % 2][:, c * 512 + jj * 128:
                                     c * 512 + jj * 128 + 128],
                    in_=ctx_tok[:, c * 128:(c + 1) * 128], transpose=True),
                    waits=([cmb, womm_h.get(g - 2)] if c == 0 else []))
            trp_h[j] = trp
            trp_guard = trp
            if pending_wo is not None:
                emit_wo(*pending_wo)
                pending_wo = None
            if jj == 3 or j == 30:
                pending_wo = (g, j)
        if pending_wo is not None and DLVL >= 4:
            emit_wo(*pending_wo)
        return [p.mark("dsync"), p.mark("vector"), p.mark("scalar"),
                p.mark("tensor"), p.mark("dscalar")]
    p1_dst = out_fm if stop_after == 1 else x1_hbm
    p1_done = ffn_pass(x_in, p1_dst, [p0_done_v, ld_bs2], parts=parts)
    if stop_after >= 2:
        attn_ready = stream_attn(waits=[p1_done[3]])
        p2_dst = out_fm if stop_after == 2 else x2_hbm
        p2_done = attn_pass(x1_hbm, p2_dst,
                            [p1_done[0], p1_done[1], p1_done[2]],
                            attn_ready, aparts=aparts)
    if stop_after >= 3:
        w1b = stream_w1(waits=[p2_done[3], p2_done[4], p2_done[1]])
        w2b = stream_w2(waits=[p2_done[3]])
        p3_done = ffn_pass(x2_hbm, out_fm,
                           [p2_done[0], p2_done[1], p2_done[2], p2_done[4]],
                           wdeps=(w1b, w2b, ws_ready))

    p._wait("sync", ("dsync", p.cnt["dsync"]))
    p._wait("sync", ("dgp", p.cnt["dgp"]))
    p.run_block()
    return nc, p



def _stage_inputs(inputs):
    f32 = np.float32
    x = np.asarray(inputs["L0_attn_hidden_states"], f32)

    def pm(v):
        return np.ascontiguousarray(np.asarray(v, f32).reshape(-1, 128).T)

    onehot8 = np.zeros((E, E * 128), ml_dtypes.bfloat16)
    for e in range(E):
        onehot8[e, e * 128:(e + 1) * 128] = 1
    mask16 = np.zeros((H, D), f32)
    for h in range(H):
        mask16[h, h * DH:(h + 1) * DH] = 1
    b1 = np.asarray(inputs["L0_ffn_b1"], f32)
    gate_W = np.asarray(inputs["L0_ffn_gate_W"], f32)

    shared = {
        "Wq": np.asarray(inputs["L0_attn_Wq"], f32),
        "Wk": np.asarray(inputs["L0_attn_Wk"], f32),
        "Wv": np.asarray(inputs["L0_attn_Wv"], f32),
        "Wo": np.asarray(inputs["L0_attn_Wo"], f32),
        "bq": pm(inputs["L0_attn_bq"]), "bk": pm(inputs["L0_attn_bk"]),
        "bo": pm(inputs["L0_attn_bo"]),
        "bv_row": np.asarray(inputs["L0_attn_bv"], f32).reshape(1, D),
        "gateW": np.ascontiguousarray(
            gate_W.reshape(NCH, 128, E).transpose(1, 0, 2).reshape(128, NCH * E)),
        "W1": np.asarray(inputs["L0_ffn_W1"], f32),
        "W2": np.asarray(inputs["L0_ffn_W2"], f32),
        "b1": np.ascontiguousarray(
            b1.reshape(E, 4, 128).transpose(2, 0, 1).reshape(128, E * 4)),
        "b2": np.asarray(inputs["L0_ffn_b2"], f32),
        "Ws1": np.asarray(inputs["L0_ffn_Ws1"], f32),
        "Ws2": np.asarray(inputs["L0_ffn_Ws2"], f32),
        "bs1": pm(inputs["L0_ffn_bs1"]), "bs2": pm(inputs["L0_ffn_bs2"]),
        "rs": np.asarray(inputs["L0_ffn_route_scale"], f32).reshape(1, 1),
        "ones_r128_f32": np.ones((1, 128), f32),
        "ones_r128_bf16": np.ones((1, 128), ml_dtypes.bfloat16),
        "ones_r8_f32": np.ones((1, 8), f32),
        "ones_c_bf16": np.ones((128, 1), ml_dtypes.bfloat16),
        "onehot8": onehot8,
        "id_f32": np.eye(128, dtype=f32),
        "mask16": mask16,
    }
    in_maps = []
    for b in range(8):
        xf = x[b].T
        xt = np.ascontiguousarray(
            xf.reshape(NCH, 128, NT, T).transpose(2, 1, 0, 3).reshape(NT, 128, NCH * T))
        m = dict(shared)
        m["x_fm"] = xt
        in_maps.append(m)
    return in_maps


def _untile(out_t):
    a = np.asarray(out_t, np.float32).reshape(NT, 128, NCH, T)
    return a.transpose(2, 1, 0, 3).reshape(D, S).T


def kernel(**inputs):
    in_maps = _stage_inputs(inputs)
    nc, _ = build_nc(stop_after=3)
    res = run_bass_kernel_spmd(nc, in_maps, core_ids=list(range(8)))
    return np.stack([_untile(res.results[b]["out_fm"]) for b in range(8)])
